# version 07
# baseline (speedup 1.0000x reference)
import contextlib

import numpy as np
import ml_dtypes

import concourse.bass as bass
from concourse.expressions import s_logical_and
import concourse.mybir as mybir
from concourse.bass_utils import run_bass_kernel_spmd

F32 = mybir.dt.float32
BF16 = mybir.dt.bfloat16
AF = mybir.ActivationFunctionType
ALU = mybir.AluOpType
AX = mybir.AxisListType
BF = ml_dtypes.bfloat16

S = 8192
D = 2048
DH = 64
NCH = 16
TOK = 1024
NCORES = 8
NEX = 33
IFF = 1024
EPS = 1e-5
ALPHA = 7.0

TT1 = 128
NT1 = S // TT1
QB = 512
NQB = S // QB
TKOPS = 17


def _nkc(qb):
    return 4 * (qb + 1)


def _g0(qb):
    return 2 * qb * (qb + 1)


def build_program(debug=False, stop_after=99):
    nc = bass.Bass("TRN2", target_bir_lowering=False)

    x_fm = nc.dram_tensor("x_fm", [D, S], F32, kind="ExternalInput")
    x_own = nc.dram_tensor("x_own", [D, TOK], F32, kind="ExternalInput")
    wq = nc.dram_tensor("wq", [D, 4 * DH], F32, kind="ExternalInput")
    wk = nc.dram_tensor("wk", [D, DH], F32, kind="ExternalInput")
    wv = nc.dram_tensor("wv", [D, DH], F32, kind="ExternalInput")
    wo = nc.dram_tensor("wo", [D, D], F32, kind="ExternalInput")
    bo_t = nc.dram_tensor("bo_t", [128, NCH], F32, kind="ExternalInput")
    rwT = nc.dram_tensor("rwT", [D, 32], F32, kind="ExternalInput")
    ebias = nc.dram_tensor("ebias", [32, 1], F32, kind="ExternalInput")
    wg_all = nc.dram_tensor("wg_all", [NEX, D, IFF], F32, kind="ExternalInput")
    wu_all = nc.dram_tensor("wu_all", [NEX, D, IFF], F32, kind="ExternalInput")
    wd_all = nc.dram_tensor("wd_all", [NEX, IFF, D], F32, kind="ExternalInput")
    cs2 = nc.dram_tensor("cs2", [2, 128, S], F32, kind="ExternalInput")
    kaug = nc.dram_tensor("kaug", [5, S], BF16, kind="ExternalInput")
    qaug = nc.dram_tensor("qaug", [4, 5, S], BF16, kind="ExternalInput")
    dm_in = nc.dram_tensor("dm_in", [128, 4, QB], BF16, kind="ExternalInput")
    rotT = nc.dram_tensor("rotT", [DH, DH], BF16, kind="ExternalInput")
    idn_f_in = nc.dram_tensor("idn_f_in", [128, 128], F32, kind="ExternalInput")
    idn_b_in = nc.dram_tensor("idn_b_in", [DH, DH], BF16, kind="ExternalInput")
    ones_f_in = nc.dram_tensor("ones_f_in", [1, 128], F32, kind="ExternalInput")
    ones_b_in = nc.dram_tensor("ones_b_in", [128, 1], BF16, kind="ExternalInput")
    vones_in = nc.dram_tensor("vones_in", [128, 64 * 65], BF16, kind="ExternalInput")
    out = nc.dram_tensor("out", [D, TOK], F32, kind="ExternalOutput")

    cc_in = nc.dram_tensor("cc_in", [4 * DH, S], BF16, kind="Internal")
    cc_out = nc.dram_tensor("cc_out", [D, S], BF16, kind="Internal")
    probs_dram = nc.dram_tensor("probs_dram", [NEX, TOK], F32, kind="Internal")

    if debug:
        dbg_kaug = nc.dram_tensor("dbg_kaug", [69, S], BF16, kind="ExternalOutput")
        dbg_qaug = nc.dram_tensor("dbg_qaug", [4, 69, S], BF16, kind="ExternalOutput")
        dbg_vaug = nc.dram_tensor("dbg_vaug", [128, 64 * 65], BF16, kind="ExternalOutput")
        dbg_ctx = nc.dram_tensor("dbg_ctx", [4, DH, S], BF16, kind="ExternalOutput")
        dbg_x2 = nc.dram_tensor("dbg_x2", [128, NCH, TOK], F32, kind="ExternalOutput")
        dbg_ctxall = nc.dram_tensor("dbg_ctxall", [128, NCH, TOK], BF16, kind="ExternalOutput")
        dbg_flat = nc.dram_tensor("dbg_flat", [128, NCH, TOK], BF16, kind="ExternalOutput")
        dbg_probs = nc.dram_tensor("dbg_probs", [32, TOK], F32, kind="ExternalOutput")

    es = contextlib.ExitStack()

    def sb(name, shape, dt, stack=None, side=None):
        return (stack or es).enter_context(nc.sbuf_tensor(name, shape, dt, side=side))

    def psum(name, shape, dt, stack=None):
        return (stack or es).enter_context(nc.psum_tensor(name, shape, dt))

    def sem(name):
        return es.enter_context(nc.semaphore(name))

    dm_sb = sb("dm_sb", [128, 4, QB], BF16)
    idn_f = sb("idn_f", [128, 128], F32)
    idn_b = sb("idn_b", [DH, DH], BF16)
    rot_sb = sb("rot_sb", [DH, DH], BF16)
    on65 = sb("on65", [65, 128], F32)
    ones_b = sb("ones_b", [128, 1], BF16)

    esA = contextlib.ExitStack()
    q_aug = [sb(f"q_aug{h}", [69, S], BF16, stack=esA) for h in range(4)]
    k_aug = sb("k_aug", [69, S], BF16, stack=esA)
    v_aug = sb("v_aug", [128, 64 * 65], BF16, stack=esA)

    x_fm3 = x_fm.ap().rearrange("(ch p) s -> p ch s", p=128)
    cs3 = cs2.ap().rearrange("c p s -> p c s")

    es1t = contextlib.ExitStack()
    xt = [sb(f"xt{i}", [128, NCH, TT1], F32, stack=es1t, side="right") for i in range(2)]
    xb = [sb(f"xb{i}", [128, NCH, TT1], BF16, stack=es1t, side="right") for i in range(2)]
    sq = [sb(f"sq{i}", [128, NCH, TT1], BF16, stack=es1t, side="right") for i in range(2)]
    ln_t = [sb(f"ln_t{i}", [128, NCH, TT1], BF16, stack=es1t, side="right") for i in range(2)]
    tmp1 = sb("tmp1", [128, NCH, TT1], F32, stack=es1t, side="right")
    cs_t = [sb(f"cs_t{i}", [128, 2, TT1], F32, stack=es1t, side="right") for i in range(2)]
    wq_sb = sb("wq_sb", [128, NCH, 4 * DH], BF16, stack=es1t, side="right")
    wk_sb = sb("wk_sb", [128, NCH, DH], BF16, stack=es1t, side="right")
    wv_sb = sb("wv_sb", [128, NCH, DH], BF16, stack=es1t, side="right")
    k_bf = sb("k_bf", [DH, TT1], BF16, stack=es1t, side="right")
    v_bf = sb("v_bf", [DH, TT1], BF16, stack=es1t, side="right")
    q_bf = [sb(f"q_bf{i}", [DH, TT1], BF16, stack=es1t, side="right") for i in range(2)]
    t1b = sb("t1b", [DH, TT1], F32, stack=es1t, side="right")
    t2b = sb("t2b", [DH, TT1], F32, stack=es1t, side="right")
    rs_row = [sb(f"rs_row{i}", [1, 2 * TT1], F32, stack=es1t, side="right") for i in range(2)]
    fin_sc = sb("fin_sc", [1, 4 * TT1], F32, stack=es1t, side="right")
    SUM_ps = psum("SUM_ps", [1, TT1], F32, stack=es1t)
    SSQ_ps = psum("SSQ_ps", [1, TT1], F32, stack=es1t)
    RS_ps = psum("RS_ps", [128, 2 * TT1], F32, stack=es1t)
    KP2 = psum("KP2", [DH, 2, TT1], F32, stack=es1t)
    VPS = psum("VPS", [DH, TT1], F32, stack=es1t)
    VT = psum("VT", [128, DH], BF16, stack=es1t)
    QPS = psum("QPS", [DH, TT1], F32, stack=es1t)
    QROT = psum("QROT", [DH, TT1], F32, stack=es1t)

    s_xt = [sem("s_xt0"), sem("s_xt1")]
    s_xb = [sem("s_xb0"), sem("s_xb1")]
    s_sq = sem("s_sq")
    s_cs = [sem("s_cs0"), sem("s_cs1")]
    s_cs_done = sem("s_cs_done")
    s_pe_st = sem("s_pe_st")
    s_fin_ve = sem("s_fin_ve")
    s_act_sd = sem("s_act_sd")
    s_fin = sem("s_fin")
    s_pe_bc1 = sem("s_pe_bc1")
    s_norm = sem("s_norm")
    s_cst = sem("s_cst")
    s_cst_g = sem("s_cst_g")
    s_pe_k = sem("s_pe_k")
    s_pe_kr = sem("s_pe_kr")
    s_pe_v = sem("s_pe_v")
    s_pe_vt = sem("s_pe_vt")
    s_pe_q = sem("s_pe_q")
    s_pe_qr = sem("s_pe_qr")
    s_d_ke = sem("s_d_ke")
    s_d_kc = sem("s_d_kc")
    s_d_ve = sem("s_d_ve")
    s_d_vt = sem("s_d_vt")
    s_d_qe = sem("s_d_qe")
    s_d_qc = sem("s_d_qc")
    s_vch = sem("s_vch")

    VPT1 = 30

    def c_fin_ve(t):
        return VPT1 * t + 5

    def c_fin(t):
        return VPT1 * t + 6

    def c_norm(t):
        return VPT1 * t + 8

    def c_ke(t):
        return VPT1 * t + 9

    def c_kc(t):
        return VPT1 * t + 10

    def c_ve(t):
        return VPT1 * t + 13

    def c_vt(t):
        return VPT1 * t + 14

    def c_qe(qi):
        return VPT1 * (qi // 4) + 15 + 4 * (qi % 4)

    def c_qc(qi):
        return c_qe(qi) + 1

    def c_csd(t):
        return VPT1 * (t + 1)

    with nc.Block() as blk:
        @blk.sync
        def _(sy):
            sy.dma_start(out=dm_sb[:], in_=dm_in.ap()).then_inc(s_cst, 16)
            sy.dma_start(out=idn_f[:], in_=idn_f_in.ap()).then_inc(s_cst, 16)
            sy.dma_start(out=idn_b[:], in_=idn_b_in.ap()).then_inc(s_cst, 16)
            sy.dma_start(out=rot_sb[:], in_=rotT.ap()).then_inc(s_cst, 16)
            sy.dma_start(out=on65[0:1, :], in_=ones_f_in.ap()).then_inc(s_cst, 16)
            sy.dma_start(out=on65[64:65, :], in_=ones_f_in.ap()).then_inc(s_cst, 16)
            sy.dma_start(out=ones_b[:], in_=ones_b_in.ap()).then_inc(s_cst, 16)
            sy.dma_start(out=v_aug[:], in_=vones_in.ap()).then_inc(s_cst, 16)
            sy.dma_start(out=k_aug[64:69, :], in_=kaug.ap()).then_inc(s_cst, 16)
            for h in range(4):
                sy.dma_start(out=q_aug[h][64:69, :], in_=qaug.ap()[h]).then_inc(s_cst, 16)
            for t in range(NT1):
                if t >= 2:
                    sy.wait_ge(s_vch, c_norm(t - 2))
                sy.dma_start(out=xt[t % 2][:], in_=x_fm3[:, :, t * TT1:(t + 1) * TT1]).then_inc(s_xt[t % 2], 16)
                if t >= 2:
                    sy.wait_ge(s_vch, c_csd(t - 2))
                sy.dma_start(out=cs_t[t % 2][:], in_=cs3[:, :, t * TT1:(t + 1) * TT1]).then_inc(s_cs[t % 2], 16)

        @blk.gpsimd
        def _(g):
            g.dma_start(out=wq_sb[:], in_=wq.ap().rearrange("(ch p) f -> p ch f", p=128)).then_inc(s_cst_g, 16)
            g.dma_start(out=wk_sb[:], in_=wk.ap().rearrange("(ch p) f -> p ch f", p=128)).then_inc(s_cst_g, 16)
            g.dma_start(out=wv_sb[:], in_=wv.ap().rearrange("(ch p) f -> p ch f", p=128)).then_inc(s_cst_g, 16)
            for t in range(NT1):
                if t >= 2:
                    g.wait_ge(s_pe_st, t - 1)
                g.dma_start(out=xb[t % 2][:], in_=x_fm3[:, :, t * TT1:(t + 1) * TT1]).then_inc(s_xb[t % 2], 16)

        @blk.scalar
        def _(sc):
            for t in range(NT1):
                sc.wait_ge(s_xt[t % 2], 16 * (t // 2 + 1))
                if t >= 2:
                    sc.wait_ge(s_pe_st, t - 1)
                sc.activation(out=sq[t % 2][:], in_=xt[t % 2][:], func=AF.Square).then_inc(s_sq, 1)
                sc.wait_ge(s_vch, c_fin_ve(t))
                sc.activation(out=fin_sc[0:1, 3 * TT1:4 * TT1], in_=fin_sc[0:1, 2 * TT1:3 * TT1],
                              func=AF.Sqrt).then_inc(s_act_sd, 1)

        @blk.tensor
        def _(te):
            te.wait_ge(s_cst, 16 * 13)
            te.wait_ge(s_cst_g, 48)
            for t in range(NT1):
                par = t % 2
                if t >= 1:
                    te.wait_ge(s_vch, VPT1 * t)
                te.wait_ge(s_xb[t % 2], 16 * (t // 2 + 1))
                te.wait_ge(s_sq, t + 1)
                if t >= 1:
                    te.wait_ge(s_vch, c_fin(t - 1))
                for ch in range(NCH):
                    te.matmul(SUM_ps[:], ones_b[:], xb[par][:, ch, :],
                              start=(ch == 0), stop=(ch == NCH - 1))
                for ch in range(NCH):
                    mm = te.matmul(SSQ_ps[:], ones_b[:], sq[par][:, ch, :],
                                   start=(ch == 0), stop=(ch == NCH - 1))
                mm.then_inc(s_pe_st, 1)
                te.wait_ge(s_vch, c_fin(t))
                if t >= 1:
                    te.wait_ge(s_vch, c_norm(t - 1))
                te.matmul(RS_ps[:], on65[0:1, :], rs_row[par][:],
                          start=True, stop=True).then_inc(s_pe_bc1, 1)
                te.wait_ge(s_vch, c_norm(t))
                if t >= 1:
                    te.wait_ge(s_vch, c_kc(t - 1))
                for ch in range(NCH):
                    mm = te.matmul(KP2[:, 0, :], wk_sb[:, ch, :], ln_t[par][:, ch, :],
                                   start=(ch == 0), stop=(ch == NCH - 1))
                mm.then_inc(s_pe_k, 1)
                te.wait_ge(s_vch, c_ke(t))
                te.matmul(KP2[:, 1, :], rot_sb[:], k_bf[:], start=True, stop=True).then_inc(s_pe_kr, 1)
                if t >= 1:
                    te.wait_ge(s_vch, c_ve(t - 1))
                for ch in range(NCH):
                    mm = te.matmul(VPS[:], wv_sb[:, ch, :], ln_t[par][:, ch, :],
                                   start=(ch == 0), stop=(ch == NCH - 1))
                mm.then_inc(s_pe_v, 1)
                if t >= 1:
                    te.wait_ge(s_vch, c_vt(t - 1))
                te.wait_ge(s_vch, c_ve(t))
                te.drain()
                te.transpose(VT[:], v_bf[:], idn_b[:])
                te.drain()
                te.nop().then_inc(s_pe_vt, 1)
                for h in range(4):
                    qi = 4 * t + h
                    if qi >= 1:
                        te.wait_ge(s_vch, c_qe(qi - 1))
                    for ch in range(NCH):
                        mm = te.matmul(QPS[:], wq_sb[:, ch, h * DH:(h + 1) * DH],
                                       ln_t[par][:, ch, :],
                                       start=(ch == 0), stop=(ch == NCH - 1))
                    mm.then_inc(s_pe_q, 1)
                    if qi >= 1:
                        te.wait_ge(s_vch, c_qc(qi - 1))
                    te.wait_ge(s_vch, c_qe(qi))
                    te.matmul(QROT[:], rot_sb[:], q_bf[qi % 2][:],
                              start=True, stop=True).then_inc(s_pe_qr, 1)

        @blk.vector
        def _(v):
            vch = [0]

            def chain(op_fn, *incs):
                if vch[0] > 0:
                    v.wait_ge(s_vch, vch[0])
                ins = op_fn()
                ins.then_inc(s_vch, 1)
                for sm, n in incs:
                    ins.then_inc(sm, n)
                vch[0] += 1

            for t in range(NT1):
                par = t % 2
                T0 = t * TT1
                if t >= 1:
                    v.wait_ge(s_pe_qr, 4 * t)
                v.wait_ge(s_pe_st, t + 1)
                if t >= 2:
                    v.wait_ge(s_pe_bc1, t - 1)
                chain(lambda par=par: v.tensor_scalar(out=rs_row[par][0:1, 0:TT1], in0=SUM_ps[:],
                                scalar1=-1.0 / D, scalar2=None, op0=ALU.mult))
                chain(lambda: v.tensor_scalar(out=fin_sc[0:1, 0:TT1], in0=SSQ_ps[:],
                                scalar1=1.0 / D, scalar2=None, op0=ALU.mult))
                chain(lambda par=par: v.tensor_tensor(out=fin_sc[0:1, TT1:2 * TT1], in0=rs_row[par][0:1, 0:TT1],
                                in1=rs_row[par][0:1, 0:TT1], op=ALU.mult))
                chain(lambda: v.tensor_tensor(out=fin_sc[0:1, 2 * TT1:3 * TT1], in0=fin_sc[0:1, 0:TT1],
                                in1=fin_sc[0:1, TT1:2 * TT1], op=ALU.subtract))
                chain(lambda: v.tensor_scalar(out=fin_sc[0:1, 2 * TT1:3 * TT1], in0=fin_sc[0:1, 2 * TT1:3 * TT1],
                                scalar1=EPS, scalar2=None, op0=ALU.add))
                v.wait_ge(s_act_sd, t + 1)
                chain(lambda par=par: v.reciprocal(rs_row[par][0:1, TT1:2 * TT1], fin_sc[0:1, 3 * TT1:4 * TT1]))
                v.wait_ge(s_pe_bc1, t + 1)
                if t >= 2:
                    v.wait_ge(s_pe_q, 4 * (t - 2) + 4)
                negmu_b = bass.AP(RS_ps, 0, [[2 * TT1, 128], [0, NCH], [1, TT1]])
                r_b = bass.AP(RS_ps, TT1, [[2 * TT1, 128], [0, NCH], [1, TT1]])
                chain(lambda par=par, negmu_b=negmu_b: v.tensor_tensor(out=tmp1[:], in0=xt[par][:], in1=negmu_b, op=ALU.add))
                chain(lambda par=par, r_b=r_b: v.tensor_tensor(out=ln_t[par][:], in0=tmp1[:], in1=r_b, op=ALU.mult))
                v.wait_ge(s_pe_k, t + 1)
                if t >= 1:
                    v.wait_ge(s_pe_kr, t)
                chain(lambda: v.tensor_copy(k_bf[:], KP2[:, 0, :]))
                v.wait_ge(s_pe_kr, t + 1)
                v.wait_ge(s_cs[t % 2], 16 * (t // 2 + 1))
                chain(lambda par=par: v.tensor_tensor(out=t1b[:], in0=KP2[:, 1, :], in1=cs_t[par][0:DH, 1, :],
                                op=ALU.mult))
                chain(lambda par=par: v.tensor_tensor(out=t2b[:], in0=k_bf[:], in1=cs_t[par][0:DH, 0, :], op=ALU.mult))
                chain(lambda T0=T0: v.tensor_tensor(out=k_aug[0:DH, T0:T0 + TT1], in0=t1b[:], in1=t2b[:], op=ALU.add))
                v.wait_ge(s_pe_v, t + 1)
                if t >= 1:
                    v.wait_ge(s_pe_vt, t)
                chain(lambda: v.tensor_copy(v_bf[:], VPS[:]))
                v.wait_ge(s_pe_vt, t + 1)
                chain(lambda t=t: v.tensor_copy(v_aug[:, t * 65:t * 65 + 64], VT[:]))
                for h in range(4):
                    qi = 4 * t + h
                    v.wait_ge(s_pe_q, qi + 1)
                    if qi >= 2:
                        v.wait_ge(s_pe_qr, qi - 1)
                    chain(lambda qi=qi: v.tensor_copy(q_bf[qi % 2][:], QPS[:]))
                    v.wait_ge(s_pe_qr, qi + 1)
                    chain(lambda par=par: v.tensor_tensor(out=t1b[:], in0=QROT[:], in1=cs_t[par][0:DH, 1, :],
                                    op=ALU.mult))
                    chain(lambda par=par, qi=qi: v.tensor_tensor(out=t2b[:], in0=q_bf[qi % 2][:], in1=cs_t[par][0:DH, 0, :],
                                    op=ALU.mult))
                    chain(lambda h=h, T0=T0: v.tensor_tensor(out=q_aug[h][0:DH, T0:T0 + TT1], in0=t1b[:], in1=t2b[:],
                                         op=ALU.add))

    es1t.close()

    if debug:
        s_dbg1 = sem("s_dbg1")
        with nc.Block() as blk:
            @blk.sync
            def _(sy):
                sy.dma_start(out=dbg_kaug.ap(), in_=k_aug[:, :]).then_inc(s_dbg1, 16)
                for h in range(4):
                    sy.dma_start(out=dbg_qaug.ap()[h], in_=q_aug[h][:, :]).then_inc(s_dbg1, 16)
                sy.dma_start(out=dbg_vaug.ap(), in_=v_aug[:, :]).then_inc(s_dbg1, 16)
                sy.wait_ge(s_dbg1, 16 * 6)


    if stop_after < 2:
        return nc

    es23 = contextlib.ExitStack()
    ctx_sb = [sb(f"ctx{h}", [DH, S], BF16, stack=es23, side="right") for h in range(4)]

    es2 = contextlib.ExitStack()
    ts_sb = sb("ts_sb", [128, 4 * QB], F32, stack=es2, side="right")
    pt_sb = [sb(f"pt{i}", [128, 4 * QB], BF16, stack=es2, side="right") for i in range(2)]
    den_sb = sb("den_sb", [65, QB], F32, stack=es2, side="right")
    rb_sb = sb("rb_sb", [DH, QB], F32, stack=es2, side="right")
    ST = psum("ST", [128, 4 * QB], F32, stack=es2)
    CTX = psum("CTX", [65, 4 * QB], F32, stack=es2)

    s_qk = sem("s_qk")
    s_tanh = sem("s_tanh")
    s_exp = sem("s_exp")
    s_mask = sem("s_mask")
    s_pv = sem("s_pv")
    s_den = sem("s_den")
    s_pe_bc2 = sem("s_pe_bc2")
    s_nrm = sem("s_nrm")
    s_vch2 = sem("s_vch2")

    def _emit_pv(te, qb, kc, G, nkc):
        diag_j = kc - 4 * qb
        if diag_j >= 0:
            te.wait_ge(s_mask, 4 * qb + diag_j + 1)
        else:
            te.wait_ge(s_exp, G + 1)
        if kc == 0 and qb > 0:
            te.wait_ge(s_nrm, 4 * qb)
        par = G % 2
        for h in range(4):
            mm = te.matmul(CTX[:, h * QB:(h + 1) * QB],
                           v_aug[:, kc * 65:kc * 65 + 65],
                           pt_sb[par][:, h * QB:(h + 1) * QB],
                           start=(kc == 0), stop=(kc == nkc - 1))
        mm.then_inc(s_pv, 1)

    with nc.Block() as blk:
        @blk.tensor
        def _(te):
            for qb in range(NQB):
                Q0 = qb * QB
                nkc = _nkc(qb)
                g0 = _g0(qb)
                for kc in range(nkc):
                    G = g0 + kc
                    if G > 0:
                        te.wait_ge(s_tanh, G)
                    if kc == 0 and qb > 0:
                        te.wait_ge(s_nrm, 4 * qb)
                    for h in range(4):
                        mm = te.matmul(ST[:, h * QB:(h + 1) * QB],
                                       k_aug[:, kc * 128:(kc + 1) * 128],
                                       q_aug[h][:, Q0:Q0 + QB], start=True, stop=True)
                    mm.then_inc(s_qk, 1)
                    if kc > 0:
                        _emit_pv(te, qb, kc - 1, G - 1, nkc)
                _emit_pv(te, qb, nkc - 1, g0 + nkc - 1, nkc)
                for h in range(4):
                    te.wait_ge(s_vch2, 8 * qb + 2 * h + 2)
                    te.matmul(ST[0:DH, h * QB:(h + 1) * QB], on65[64:65, 0:DH],
                              den_sb[64:65, :], start=True, stop=True).then_inc(s_pe_bc2, 1)

        @blk.scalar
        def _(sc):
            for qb in range(NQB):
                nkc = _nkc(qb)
                g0 = _g0(qb)
                for kc in range(nkc):
                    G = g0 + kc
                    sc.wait_ge(s_qk, G + 1)
                    sc.activation(out=ts_sb[:], in_=ST[:], func=AF.Tanh,
                                  scale=1.0 / 240.0).then_inc(s_tanh, 1)
                    if G >= 2:
                        sc.wait_ge(s_pv, G - 1)
                    sc.activation(out=pt_sb[G % 2][:], in_=ts_sb[:], func=AF.Exp,
                                  scale=30.0).then_inc(s_exp, 1)

        @blk.vector
        def _(v):
            for qb in range(NQB):
                Q0 = qb * QB
                nkc = _nkc(qb)
                g0 = _g0(qb)
                for j in range(4):
                    G = g0 + 4 * qb + j
                    v.wait_ge(s_exp, G + 1)
                    par = G % 2
                    pt3 = bass.AP(pt_sb[par], 0, [[4 * QB, 128], [QB, 4], [1, QB]])
                    dmj = bass.AP(dm_sb, j * QB, [[4 * QB, 128], [0, 4], [1, QB]])
                    v.tensor_tensor(out=pt3, in0=pt3, in1=dmj, op=ALU.mult).then_inc(s_mask, 1)
                v.wait_ge(s_pv, g0 + nkc)
                for h in range(4):
                    v.tensor_copy(den_sb[64:65, :], CTX[64:65, h * QB:(h + 1) * QB]).then_inc(s_vch2, 1)
                    v.wait_ge(s_vch2, 8 * qb + 2 * h + 1)
                    v.reciprocal(den_sb[64:65, :], den_sb[64:65, :]).then_inc(s_vch2, 1)
                    v.wait_ge(s_pe_bc2, 4 * qb + h + 1)
                    v.tensor_copy(rb_sb[:], ST[0:DH, h * QB:(h + 1) * QB])
                    v.tensor_tensor(out=ctx_sb[h][:, Q0:Q0 + QB],
                                    in0=CTX[0:DH, h * QB:(h + 1) * QB],
                                    in1=rb_sb[:], op=ALU.mult).then_inc(s_nrm, 1)

    es2.close()


    if stop_after < 3:
        return nc

    s_cc1 = sem("s_cc1")
    s_cc2 = sem("s_cc2")
    with nc.Block() as blk:
        @blk.gpsimd
        def _(g):
            for h in range(4):
                g.dma_start(out=cc_in.ap()[h * DH:(h + 1) * DH, :], in_=ctx_sb[h][:, :]).then_inc(s_cc1, 16)
            g.wait_ge(s_cc1, 64)
            g.collective_compute(
                "AllGather", ALU.bypass,
                replica_groups=[list(range(NCORES))],
                ins=[cc_in.ap()], outs=[cc_out.ap()],
            ).then_inc(s_cc2, 1)
            g.wait_ge(s_cc2, 1)

    if debug:
        s_dbg2 = sem("s_dbg2")
        with nc.Block() as blk:
            @blk.sync
            def _(sy):
                for h in range(4):
                    sy.dma_start(out=dbg_ctx.ap()[h], in_=ctx_sb[h][:, :]).then_inc(s_dbg2, 16)
                sy.wait_ge(s_dbg2, 64)

    es23.close()
    esA.close()


    if stop_after < 4:
        return nc

    es4 = contextlib.ExitStack()
    x2 = sb("x2", [128, NCH, TOK], F32, stack=es4, side="right")
    flat = sb("flat", [128, NCH, TOK], BF16, stack=es4, side="right")

    es4a1 = contextlib.ExitStack()
    ctx_all = sb("ctx_all", [128, NCH, TOK], BF16, stack=es4a1, side="right")
    es4a2 = contextlib.ExitStack()
    wo_c = [sb(f"wo_c{i}", [128, NCH, 128], BF16, stack=es4a2, side="right") for i in range(2)]
    bo_sb = sb("bo_sb", [128, NCH], F32, stack=es4a2, side="right")
    ot_sb = sb("ot_sb", [128, QB], F32, stack=es4a2, side="right")
    OPS = psum("OPS", [128, 2, TOK], F32, stack=es4a2)

    s_ctxa = sem("s_ctxa")
    s_xo = sem("s_xo")
    s_wo = [sem("s_wo0"), sem("s_wo1")]
    s_pe_o = sem("s_pe_o")
    s_d_o = sem("s_d_o")
    s_b4c = sem("s_b4c")
    s_xb2 = sem("s_xb2")

    cc_out3 = cc_out.ap().rearrange("(ch p) s -> p ch s", p=128)
    wo3 = wo.ap().rearrange("(ch p) d -> p ch d", p=128)

    with nc.Block() as blk:
        @blk.sync
        def _(sy):
            pid = sy.snap(sy.partition_id(), min_val=0, max_val=NCORES - 1)
            sy.dma_start(out=ctx_all[:],
                         in_=cc_out3[:, :, bass.ds(pid * TOK, TOK)]).then_inc(s_ctxa, 16)
            sy.dma_start(out=x2[:], in_=x_own.ap().rearrange("(ch p) t -> p ch t", p=128)).then_inc(s_xo, 16)
            sy.dma_start(out=bo_sb[:], in_=bo_t.ap()).then_inc(s_b4c, 16)
            if debug:
                sy.wait_ge(s_ctxa, 16)
                sy.dma_start(out=dbg_ctxall.ap(), in_=ctx_all[:]).then_inc(s_b4c, 16)

        @blk.gpsimd
        def _(g):
            for dc in range(NCH):
                if dc >= 2:
                    g.wait_ge(s_pe_o, 2 * (dc - 1))
                g.dma_start(out=wo_c[dc % 2][:], in_=wo3[:, :, dc * 128:(dc + 1) * 128]).then_inc(s_wo[dc % 2], 16)

        @blk.tensor
        def _(te):
            te.wait_ge(s_ctxa, 16)
            for dc in range(NCH):
                te.wait_ge(s_wo[dc % 2], 16 * (dc // 2 + 1))
                if dc >= 2:
                    te.wait_ge(s_d_o, 2 * (dc - 1))
                for tb in range(2):
                    for ch in range(NCH):
                        mm = te.matmul(OPS[:, dc % 2, tb * QB:(tb + 1) * QB],
                                       wo_c[dc % 2][:, ch, :],
                                       ctx_all[:, ch, tb * QB:(tb + 1) * QB],
                                       start=(ch == 0), stop=(ch == NCH - 1))
                    mm.then_inc(s_pe_o, 1)

        @blk.vector
        def _(v):
            v.wait_ge(s_xo, 16)
            v.wait_ge(s_b4c, 16)
            for dc in range(NCH):
                for tb in range(2):
                    v.wait_ge(s_pe_o, 2 * dc + tb + 1)
                    v.tensor_scalar(out=ot_sb[:], in0=OPS[:, dc % 2, tb * QB:(tb + 1) * QB],
                                    scalar1=bo_sb[:, dc:dc + 1], scalar2=None, op0=ALU.add)
                    v.tensor_tensor(out=x2[:, dc, tb * QB:(tb + 1) * QB],
                                    in0=x2[:, dc, tb * QB:(tb + 1) * QB],
                                    in1=ot_sb[:], op=ALU.add).then_inc(s_d_o, 1)
            v.tensor_copy(ctx_all[:], x2[:]).then_inc(s_xb2, 1)

    es4a2.close()


    if stop_after < 5:
        return nc

    es4b = contextlib.ExitStack()
    sq2 = sb("sq2", [128, NCH, TOK], BF16, stack=es4b, side="right")
    tmp4 = sb("tmp4", [128, TOK], F32, stack=es4b, side="right")
    rs2_row = sb("rs2_row", [1, 2 * TOK], F32, stack=es4b, side="right")
    fin2 = sb("fin2", [1, 4 * TOK], F32, stack=es4b, side="right")
    SUM2 = psum("SUM2", [1, TOK], F32, stack=es4b)
    SSQ2 = psum("SSQ2", [1, TOK], F32, stack=es4b)
    RS2 = psum("RS2", [128, 2 * TOK], F32, stack=es4b)

    s_sq2 = sem("s_sq2")
    s_pe_st2 = sem("s_pe_st2")
    s_fin2_ve = sem("s_fin2_ve")
    s_act_sd2 = sem("s_act_sd2")
    s_fin2 = sem("s_fin2")
    s_pe_bc4 = sem("s_pe_bc4")
    s_norm2 = sem("s_norm2")
    s_vch4 = sem("s_vch4")

    with nc.Block() as blk:
        @blk.scalar
        def _(sc):
            sc.activation(out=sq2[:], in_=x2[:], func=AF.Square).then_inc(s_sq2, 1)
            sc.wait_ge(s_fin2_ve, 1)
            sc.activation(out=fin2[0:1, 3 * TOK:4 * TOK], in_=fin2[0:1, 2 * TOK:3 * TOK],
                          func=AF.Sqrt).then_inc(s_act_sd2, 1)

        @blk.tensor
        def _(te):
            for tb in range(2):
                for ch in range(NCH):
                    mm = te.matmul(SUM2[0:1, tb * QB:(tb + 1) * QB], ones_b[:],
                                   ctx_all[:, ch, tb * QB:(tb + 1) * QB],
                                   start=(ch == 0), stop=(ch == NCH - 1))
            mm.then_inc(s_pe_st2, 1)
            te.wait_ge(s_sq2, 1)
            for tb in range(2):
                for ch in range(NCH):
                    mm = te.matmul(SSQ2[0:1, tb * QB:(tb + 1) * QB], ones_b[:],
                                   sq2[:, ch, tb * QB:(tb + 1) * QB],
                                   start=(ch == 0), stop=(ch == NCH - 1))
            mm.then_inc(s_pe_st2, 1)
            te.wait_ge(s_fin2, 1)
            for j in range(4):
                te.matmul(RS2[:, j * QB:(j + 1) * QB], on65[0:1, :],
                          rs2_row[0:1, j * QB:(j + 1) * QB],
                          start=True, stop=True).then_inc(s_pe_bc4, 1)

        @blk.vector
        def _(v):
            v.wait_ge(s_pe_st2, 2)
            v.tensor_scalar(out=rs2_row[0:1, 0:TOK], in0=SUM2[:], scalar1=-1.0 / D,
                            scalar2=None, op0=ALU.mult).then_inc(s_vch4, 1)
            v.tensor_scalar(out=fin2[0:1, 0:TOK], in0=SSQ2[:], scalar1=1.0 / D,
                            scalar2=None, op0=ALU.mult).then_inc(s_vch4, 1)
            v.wait_ge(s_vch4, 2)
            v.tensor_tensor(out=fin2[0:1, TOK:2 * TOK], in0=rs2_row[0:1, 0:TOK],
                            in1=rs2_row[0:1, 0:TOK], op=ALU.mult).then_inc(s_vch4, 1)
            v.wait_ge(s_vch4, 3)
            v.tensor_tensor(out=fin2[0:1, 2 * TOK:3 * TOK], in0=fin2[0:1, 0:TOK],
                            in1=fin2[0:1, TOK:2 * TOK], op=ALU.subtract).then_inc(s_vch4, 1)
            v.wait_ge(s_vch4, 4)
            v.tensor_scalar(out=fin2[0:1, 2 * TOK:3 * TOK], in0=fin2[0:1, 2 * TOK:3 * TOK],
                            scalar1=EPS, scalar2=None, op0=ALU.add).then_inc(s_fin2_ve, 1)
            v.wait_ge(s_act_sd2, 1)
            v.reciprocal(rs2_row[0:1, TOK:2 * TOK], fin2[0:1, 3 * TOK:4 * TOK]).then_inc(s_fin2, 1)
            v.wait_ge(s_pe_bc4, 4)
            for ch in range(NCH):
                v.tensor_tensor(out=tmp4[:], in0=x2[:, ch, :], in1=RS2[:, 0:TOK], op=ALU.add)
                v.tensor_tensor(out=flat[:, ch, :], in0=tmp4[:], in1=RS2[:, TOK:2 * TOK],
                                op=ALU.mult).then_inc(s_norm2, 1)

    es4b.close()
    es4a1.close()


    if stop_after < 6:
        return nc

    es4c = contextlib.ExitStack()
    rw_sb = sb("rw_sb", [128, NCH, 32], BF16, stack=es4c, side="right")
    eb_sb = sb("eb_sb", [32, 1], F32, stack=es4c, side="right")
    sp_a = sb("sp_a", [32, TOK], F32, stack=es4c, side="right")
    sp_r = sb("sp_r", [32, TOK], F32, stack=es4c, side="right")
    sc_sb = sb("sc_sb", [32, TOK], F32, stack=es4c, side="right")
    y_sb = sb("y_sb", [32, TOK], F32, stack=es4c, side="right")
    sc_t = sb("sc_t", [128, 8, 32], F32, stack=es4c, side="right")
    y_t = sb("y_t", [128, 8, 32], F32, stack=es4c, side="right")
    yc_t = sb("yc_t", [128, 8, 32], F32, stack=es4c, side="right")
    et_t = sb("et_t", [128, 32], F32, stack=es4c, side="right")
    mx_t = sb("mx_t", [128, 2], F32, stack=es4c, side="right")
    pr_t = sb("pr_t", [128, 8, 32], F32, stack=es4c, side="right")
    prb = sb("prb", [32, TOK], F32, stack=es4c, side="right")
    ones_row = sb("ones_row", [1, TOK], F32, stack=es4c, side="right")
    RPS = psum("RPS", [32, TOK], F32, stack=es4c)
    TPSa = psum("TPSa", [128, 32], F32, stack=es4c)
    TPSb = psum("TPSb", [128, 32], F32, stack=es4c)
    TPS2 = psum("TPS2", [32, 128], F32, stack=es4c)

    s_b4c2 = sem("s_b4c2")
    s_pe_r = sem("s_pe_r")
    s_sp1 = sem("s_sp1")
    s_spv = sem("s_spv")
    s_scs = sem("s_scs")
    s_yv = sem("s_yv")
    s_pe_t1 = sem("s_pe_t1")
    s_d_t1 = sem("s_d_t1")
    s_tk = sem("s_tk")
    s_pe_t2 = sem("s_pe_t2")
    s_d_t2 = sem("s_d_t2")
    s_prd = sem("s_prd")

    with nc.Block() as blk:
        @blk.gpsimd
        def _(g):
            g.dma_start(out=rw_sb[:], in_=rwT.ap().rearrange("(ch p) e -> p ch e", p=128)).then_inc(s_b4c2, 16)
            g.dma_start(out=eb_sb[:], in_=ebias.ap()).then_inc(s_b4c2, 16)

        @blk.tensor
        def _(te):
            te.wait_ge(s_b4c2, 32)
            for tb in range(2):
                for ch in range(NCH):
                    mm = te.matmul(RPS[:, tb * QB:(tb + 1) * QB], rw_sb[:, ch, :],
                                   flat[:, ch, tb * QB:(tb + 1) * QB],
                                   start=(ch == 0), stop=(ch == NCH - 1))
                mm.then_inc(s_pe_r, 1)
            te.wait_ge(s_yv, 1)
            for b in range(8):
                if b >= 1:
                    te.wait_ge(s_d_t1, b)
                te.transpose(TPSa[:], sc_sb[:, b * 128:(b + 1) * 128], idn_f[0:32, 0:32])
                te.transpose(TPSb[:], y_sb[:, b * 128:(b + 1) * 128],
                             idn_f[0:32, 0:32]).then_inc(s_pe_t1, 1)
            for b in range(8):
                te.wait_ge(s_tk, TKOPS * (b + 1))
                if b >= 1:
                    te.wait_ge(s_d_t2, b)
                te.transpose(TPS2[:], pr_t[:, b, :], idn_f[:, :]).then_inc(s_pe_t2, 1)

        @blk.scalar
        def _(sc):
            sc.wait_ge(s_pe_r, 2)
            sc.activation(out=sp_a[:], in_=RPS[:], func=AF.Abs)
            sc.activation(out=sp_a[:], in_=sp_a[:], func=AF.Exp, scale=-1.0)
            sc.activation(out=sp_a[:], in_=sp_a[:], func=AF.Ln, bias=1.0)
            sc.activation(out=sp_r[:], in_=RPS[:], func=AF.Relu).then_inc(s_sp1, 1)
            sc.wait_ge(s_spv, 1)
            sc.activation(out=sc_sb[:], in_=sc_sb[:], func=AF.Sqrt).then_inc(s_scs, 1)

        @blk.vector
        def _(v):
            v.wait_ge(s_sp1, 1)
            v.tensor_tensor(out=sc_sb[:], in0=sp_a[:], in1=sp_r[:], op=ALU.add).then_inc(s_spv, 1)
            v.wait_ge(s_scs, 1)
            v.wait_ge(s_b4c2, 32)
            v.tensor_scalar(out=y_sb[:], in0=sc_sb[:], scalar1=eb_sb[:, 0:1], scalar2=None,
                            op0=ALU.add).then_inc(s_yv, 1)
            for b in range(8):
                v.wait_ge(s_pe_t1, b + 1)
                v.tensor_copy(sc_t[:, b, :], TPSa[:])
                v.tensor_copy(y_t[:, b, :], TPSb[:]).then_inc(s_d_t1, 1)
            tk = [0]

            def step(op_fn):
                if tk[0] > 0:
                    v.wait_ge(s_tk, tk[0])
                op_fn().then_inc(s_tk, 1)
                tk[0] += 1

            for b in range(8):
                yb = y_t[:, b, :]
                step(lambda b=b, yb=yb: v.tensor_copy(yc_t[:, b, :], yb))
                for it in range(4):
                    dst = mx_t[:, 0:1] if it < 3 else mx_t[:, 1:2]
                    step(lambda dst=dst, yb=yb: v.tensor_reduce(dst, yb, axis=AX.X, op=ALU.max))
                    if it < 3:
                        step(lambda yb=yb: v.tensor_scalar(out=et_t[:], in0=yb, scalar1=mx_t[:, 0:1],
                                                           scalar2=None, op0=ALU.is_equal))
                        step(lambda yb=yb: v.scalar_tensor_tensor(out=yb, in0=et_t[:], scalar=-1e30,
                                                                  in1=yb, op0=ALU.mult, op1=ALU.add))
                step(lambda b=b: v.tensor_scalar(out=et_t[:], in0=yc_t[:, b, :], scalar1=mx_t[:, 1:2],
                                                 scalar2=None, op0=ALU.is_ge))
                step(lambda b=b: v.tensor_tensor(out=pr_t[:, b, :], in0=et_t[:], in1=sc_t[:, b, :],
                                                 op=ALU.mult))
                step(lambda b=b: v.tensor_reduce(mx_t[:, 0:1], pr_t[:, b, :], axis=AX.X, op=ALU.add))
                step(lambda: v.tensor_scalar(out=mx_t[:, 0:1], in0=mx_t[:, 0:1], scalar1=1e-20,
                                             scalar2=None, op0=ALU.add))
                step(lambda: v.reciprocal(mx_t[:, 0:1], mx_t[:, 0:1]))
                step(lambda b=b: v.tensor_scalar(out=pr_t[:, b, :], in0=pr_t[:, b, :],
                                                 scalar1=mx_t[:, 0:1], scalar2=None, op0=ALU.mult))
            assert tk[0] == TKOPS * 8, tk[0]
            for b in range(8):
                v.wait_ge(s_pe_t2, b + 1)
                v.tensor_copy(prb[:, b * 128:(b + 1) * 128], TPS2[:]).then_inc(s_d_t2, 1)
            v.memset(ones_row[:], 1.0).then_inc(s_d_t2, 1)

        @blk.sync
        def _(sy):
            sy.wait_ge(s_d_t2, 9)
            sy.dma_start(out=probs_dram.ap()[0:32, :], in_=prb[:, :]).then_inc(s_prd, 16)
            sy.dma_start(out=probs_dram.ap()[32:33, :], in_=ones_row[:, :]).then_inc(s_prd, 16)
            sy.wait_ge(s_prd, 32)

    es4c.close()

    if debug:
        s_dbg4 = sem("s_dbg4")
        with nc.Block() as blk:
            @blk.sync
            def _(sy):
                sy.dma_start(out=dbg_x2.ap(), in_=x2[:]).then_inc(s_dbg4, 16)
                sy.dma_start(out=dbg_flat.ap(), in_=flat[:]).then_inc(s_dbg4, 16)
                sy.dma_start(out=dbg_probs.ap(), in_=probs_dram.ap()[0:32, :]).then_inc(s_dbg4, 16)
                sy.wait_ge(s_dbg4, 48)


    if stop_after < 7:
        return nc

    es5 = contextlib.ExitStack()
    h_sb = [sb(f"h_sb{i}", [128, 8, TOK], BF16, stack=es5, side="right") for i in range(2)]
    wgc = [sb(f"wgc{i}", [128, NCH, 128], BF16, stack=es5, side="right") for i in range(2)]
    wuc = [sb(f"wuc{i}", [128, NCH, 128], BF16, stack=es5, side="right") for i in range(2)]
    wdc = [sb(f"wdc{i}", [128, 8, 128], BF16, stack=es5, side="right") for i in range(2)]
    gc_b = [sb(f"gc_b{i}", [128, TOK], F32, stack=es5, side="right") for i in range(2)]
    sg_b = [sb(f"sg_b{i}", [128, TOK], F32, stack=es5, side="right") for i in range(2)]
    uc_b = sb("uc_b", [128, TOK], F32, stack=es5, side="right")
    tt_b = sb("tt_b", [128, TOK], F32, stack=es5, side="right")
    p_bc = [sb(f"p_bc{i}", [128, TOK], F32, stack=es5, side="right") for i in range(2)]
    GPS = psum("GPS", [128, TOK], F32, stack=es5)
    UPS = psum("UPS", [128, TOK], F32, stack=es5)
    DPS = psum("DPS", [128, 2, TOK], F32, stack=es5)

    s_wg = [sem("s_wg0"), sem("s_wg1")]
    s_wu = [sem("s_wu0"), sem("s_wu1")]
    s_wd = [sem("s_wd0"), sem("s_wd1")]
    s_pbc = [sem("s_pbc0"), sem("s_pbc1")]
    s_peg = sem("s_peg")
    s_peu = sem("s_peu")
    s_ped = sem("s_ped")
    s_gc = sem("s_gc")
    s_uc = sem("s_uc")
    s_sg = sem("s_sg")
    s_h = sem("s_h")
    s_x2o = sem("s_x2o")
    s_out = sem("s_out")

    NDC = NEX * 16

    with nc.Block() as blk:
        @blk.gpsimd
        def _(g):
            for e in range(NEX + 1):
                if e < NEX:
                    wg3 = wg_all.ap()[e].rearrange("(ch p) f -> p ch f", p=128)
                    wu3 = wu_all.ap()[e].rearrange("(ch p) f -> p ch f", p=128)
                    for fi in range(8):
                        gidx = e * 8 + fi
                        if gidx >= 2:
                            g.wait_ge(s_peg, gidx - 1)
                        g.dma_start(out=wgc[gidx % 2][:], in_=wg3[:, :, fi * 128:(fi + 1) * 128]).then_inc(s_wg[gidx % 2], 16)
                        if gidx >= 2:
                            g.wait_ge(s_peu, gidx - 1)
                        g.dma_start(out=wuc[gidx % 2][:], in_=wu3[:, :, fi * 128:(fi + 1) * 128]).then_inc(s_wu[gidx % 2], 16)
                if e >= 1:
                    wd3 = wd_all.ap()[e - 1].rearrange("(f p) d -> p f d", p=128)
                    for dc in range(16):
                        didx = (e - 1) * 16 + dc
                        if didx >= 2:
                            g.wait_ge(s_ped, didx - 1)
                        g.dma_start(out=wdc[didx % 2][:], in_=wd3[:, :, dc * 128:(dc + 1) * 128]).then_inc(s_wd[didx % 2], 16)

        @blk.sync
        def _(sy):
            for e in range(NEX):
                if e >= 2:
                    sy.wait_ge(s_h, 8 * (e - 1))
                pb_ap = bass.AP(probs_dram, e * TOK, [[0, 128], [1, TOK]])
                sy.dma_start(out=p_bc[e % 2][:], in_=pb_ap).then_inc(s_pbc[e % 2], 16)
            sy.wait_ge(s_x2o, NDC)
            sy.dma_start(out=out.ap().rearrange("(ch p) t -> p ch t", p=128), in_=x2[:]).then_inc(s_out, 16)
            sy.wait_ge(s_out, 16)

        @blk.tensor
        def _(te):
            for e in range(NEX + 1):
                if e < NEX:
                    for fi in range(8):
                        gidx = e * 8 + fi
                        te.wait_ge(s_wg[gidx % 2], 16 * (gidx // 2 + 1))
                        if gidx >= 1:
                            te.wait_ge(s_gc, gidx)
                        for tb in range(2):
                            for ch in range(NCH):
                                mm = te.matmul(GPS[:, tb * QB:(tb + 1) * QB], wgc[gidx % 2][:, ch, :],
                                               flat[:, ch, tb * QB:(tb + 1) * QB],
                                               start=(ch == 0), stop=(ch == NCH - 1))
                        mm.then_inc(s_peg, 1)
                        te.wait_ge(s_wu[gidx % 2], 16 * (gidx // 2 + 1))
                        if gidx >= 1:
                            te.wait_ge(s_uc, gidx)
                        for tb in range(2):
                            for ch in range(NCH):
                                mm = te.matmul(UPS[:, tb * QB:(tb + 1) * QB], wuc[gidx % 2][:, ch, :],
                                               flat[:, ch, tb * QB:(tb + 1) * QB],
                                               start=(ch == 0), stop=(ch == NCH - 1))
                        mm.then_inc(s_peu, 1)
                if e >= 1:
                    ep = e - 1
                    for dc in range(16):
                        didx = ep * 16 + dc
                        te.wait_ge(s_wd[didx % 2], 16 * (didx // 2 + 1))
                        te.wait_ge(s_h, 8 * (ep + 1))
                        if didx >= 2:
                            te.wait_ge(s_x2o, didx - 1)
                        for tb in range(2):
                            for fi in range(8):
                                mm = te.matmul(DPS[:, didx % 2, tb * QB:(tb + 1) * QB],
                                               wdc[didx % 2][:, fi, :],
                                               h_sb[ep % 2][:, fi, tb * QB:(tb + 1) * QB],
                                               start=(fi == 0), stop=(fi == 7))
                        mm.then_inc(s_ped, 1)

        @blk.scalar
        def _(sc):
            for e in range(NEX):
                for fi in range(8):
                    gidx = e * 8 + fi
                    sc.wait_ge(s_gc, gidx + 1)
                    if gidx >= 2:
                        sc.wait_ge(s_h, gidx - 1)
                    sc.activation(out=sg_b[gidx % 2][:], in_=gc_b[gidx % 2][:],
                                  func=AF.Silu).then_inc(s_sg, 1)

        @blk.vector
        def _(v):
            for e in range(NEX + 1):
                if e < NEX:
                    for fi in range(8):
                        gidx = e * 8 + fi
                        v.wait_ge(s_peg, gidx + 1)
                        if gidx >= 2:
                            v.wait_ge(s_sg, gidx - 1)
                        v.tensor_scalar(out=gc_b[gidx % 2][:], in0=GPS[:], scalar1=ALPHA,
                                        scalar2=-ALPHA, op0=ALU.min, op1=ALU.max).then_inc(s_gc, 1)
                        v.wait_ge(s_peu, gidx + 1)
                        v.tensor_scalar(out=uc_b[:], in0=UPS[:], scalar1=ALPHA,
                                        scalar2=-ALPHA, op0=ALU.min, op1=ALU.max).then_inc(s_uc, 1)
                        v.wait_ge(s_sg, gidx + 1)
                        v.tensor_tensor(out=tt_b[:], in0=sg_b[gidx % 2][:], in1=uc_b[:], op=ALU.mult)
                        v.wait_ge(s_pbc[e % 2], 16 * (e // 2 + 1))
                        if e >= 2:
                            v.wait_ge(s_ped, 16 * (e - 1))
                        v.tensor_tensor(out=h_sb[e % 2][:, fi, :], in0=tt_b[:],
                                        in1=p_bc[e % 2][:], op=ALU.mult).then_inc(s_h, 1)
                if e >= 1:
                    ep = e - 1
                    for dc in range(16):
                        didx = ep * 16 + dc
                        v.wait_ge(s_ped, didx + 1)
                        v.tensor_tensor(out=x2[:, dc, :], in0=x2[:, dc, :],
                                        in1=DPS[:, didx % 2, :], op=ALU.add).then_inc(s_x2o, 1)

    es5.close()
    es4.close()
    es.close()
    return nc


def _consts():
    pos = np.arange(S, dtype=np.float64)
    inv = (1.0 / 10000.0 ** (np.arange(0, DH, 2, dtype=np.float32) / DH)).astype(np.float32)
    fr = pos.astype(np.float32)[:, None] * inv[None, :]
    cosv = np.cos(fr).astype(np.float32)
    sinv = np.sin(fr).astype(np.float32)
    ridx = (np.arange(128) % DH) // 2
    cs2 = np.stack([cosv[:, ridx].T, sinv[:, ridx].T], axis=0)
    cs2 = np.ascontiguousarray(cs2, dtype=np.float32)
    k_hi = pos.astype(BF)
    k_lo = (pos - k_hi.astype(np.float64)).astype(BF)
    neg1 = np.full(S, -1.0, dtype=BF)
    kaug = np.stack([k_hi, k_hi, k_lo, neg1, neg1], axis=0)
    r = np.arange(128)[:, None]
    c = np.arange(QB)[None, :]
    dm = np.stack([(128 * o + r <= c) for o in range(4)], axis=1).astype(BF)
    R = np.zeros((DH, DH), dtype=np.float32)
    for i in range(DH // 2):
        R[2 * i, 2 * i + 1] = -1.0
        R[2 * i + 1, 2 * i] = 1.0
    rotT = np.ascontiguousarray(R.T).astype(BF)
    idn_f = np.eye(128, dtype=np.float32)
    idn_b = np.eye(DH, dtype=np.float32).astype(BF)
    ones_f = np.ones((1, 128), dtype=np.float32)
    ones_b = np.ones((128, 1), dtype=np.float32).astype(BF)
    vones = np.zeros((128, 64 * 65), dtype=np.float32)
    vones[:, 64::65] = 1.0
    vones = vones.astype(BF)
    return dict(cs2=cs2, kaug=np.ascontiguousarray(kaug), dm_in=np.ascontiguousarray(dm),
                rotT=rotT, idn_f_in=idn_f, idn_b_in=idn_b, ones_f_in=ones_f,
                ones_b_in=ones_b, vones_in=vones)


def _qaug_for_core(cid):
    pos = np.arange(S, dtype=np.float64)
    rows = []
    for j in range(4):
        hg = 4 * cid + j
        slope = 2.0 ** (-(hg + 1) * 8.0 / 32.0)
        s_hi_b = np.array(slope, dtype=BF)
        s_lo_b = np.array(slope - s_hi_b.astype(np.float64), dtype=BF)
        vv = slope * pos
        v_hi = vv.astype(BF)
        v_lo = (vv - v_hi.astype(np.float64)).astype(BF)
        rows.append(np.stack([
            np.full(S, s_hi_b, dtype=BF),
            np.full(S, s_lo_b, dtype=BF),
            np.full(S, s_hi_b, dtype=BF),
            v_hi, v_lo], axis=0))
    return np.ascontiguousarray(np.stack(rows, axis=0))


_IN_MAPS_HOOK = None


def kernel(L0_attn_hidden_states, L0_attn_Wq, L0_attn_Wk, L0_attn_Wv, L0_attn_Wo,
           L0_attn_bo, L0_ffn_router_w, L0_ffn_expert_bias, L0_ffn_gate_w, L0_ffn_up_w,
           L0_ffn_down_w, L0_ffn_shared_gate_w, L0_ffn_shared_up_w, L0_ffn_shared_down_w):
    f32 = np.float32
    x = np.asarray(L0_attn_hidden_states, f32).reshape(S, D)
    Wq = np.asarray(L0_attn_Wq, f32)
    Wk = np.asarray(L0_attn_Wk, f32)
    Wv = np.asarray(L0_attn_Wv, f32)
    Wo = np.asarray(L0_attn_Wo, f32)
    bo = np.asarray(L0_attn_bo, f32)
    rw = np.asarray(L0_ffn_router_w, f32)
    eb = np.asarray(L0_ffn_expert_bias, f32)
    gw = np.asarray(L0_ffn_gate_w, f32)
    uw = np.asarray(L0_ffn_up_w, f32)
    dw = np.asarray(L0_ffn_down_w, f32)
    sg = np.asarray(L0_ffn_shared_gate_w, f32)
    su = np.asarray(L0_ffn_shared_up_w, f32)
    sd = np.asarray(L0_ffn_shared_down_w, f32)

    x_fm = np.ascontiguousarray(x.T)
    wo_full = np.ascontiguousarray(Wo)
    bo_t2 = np.ascontiguousarray(bo.reshape(NCH, 128).T)
    rwT2 = np.ascontiguousarray(rw.T)
    eb2 = np.ascontiguousarray(eb.reshape(32, 1))
    wg_T = np.ascontiguousarray(np.concatenate([gw.transpose(0, 2, 1), sg.T[None]], axis=0))
    wu_T = np.ascontiguousarray(np.concatenate([uw.transpose(0, 2, 1), su.T[None]], axis=0))
    wd_T = np.ascontiguousarray(np.concatenate([dw.transpose(0, 2, 1), sd.T[None]], axis=0))

    shared = _consts()
    shared.update(x_fm=x_fm, wo=wo_full, bo_t=bo_t2, rwT=rwT2, ebias=eb2,
                  wg_all=wg_T, wu_all=wu_T, wd_all=wd_T)

    in_maps = []
    for c in range(NCORES):
        m = dict(shared)
        m["x_own"] = np.ascontiguousarray(x_fm[:, c * TOK:(c + 1) * TOK])
        m["wq"] = np.ascontiguousarray(Wq[:, c * 4 * DH:(c + 1) * 4 * DH])
        m["wk"] = np.ascontiguousarray(Wk[:, c * DH:(c + 1) * DH])
        m["wv"] = np.ascontiguousarray(Wv[:, c * DH:(c + 1) * DH])
        m["qaug"] = _qaug_for_core(c)
        in_maps.append(m)
    if _IN_MAPS_HOOK is not None:
        return _IN_MAPS_HOOK(in_maps)

    nc = build_program(debug=False)
    res = run_bass_kernel_spmd(nc, in_maps, core_ids=list(range(NCORES)))
    parts = [np.asarray(res.results[c]["out"], dtype=np.float32).T for c in range(NCORES)]
    try:
        nn = [int(np.isnan(p).sum()) for p in parts]
        if any(nn):
            print("KNAN per-core:", nn, flush=True)
            np.save("/tmp/knan_out.npy", np.stack([p[:8, :8] for p in parts]))
    except Exception:
        pass
    return np.concatenate(parts, axis=0).reshape(1, S, D)


# version 08
# speedup vs baseline: 1.0174x; 1.0174x over previous
import contextlib

import numpy as np
import ml_dtypes

import concourse.bass as bass
from concourse.expressions import s_logical_and
import concourse.mybir as mybir
from concourse.bass_utils import run_bass_kernel_spmd

F32 = mybir.dt.float32
BF16 = mybir.dt.bfloat16
AF = mybir.ActivationFunctionType
ALU = mybir.AluOpType
AX = mybir.AxisListType
BF = ml_dtypes.bfloat16

S = 8192
D = 2048
DH = 64
NCH = 16
TOK = 1024
NCORES = 8
NEX = 33
IFF = 1024
EPS = 1e-5
ALPHA = 7.0

TT1 = 128
NT1 = S // TT1
QB = 512
NQB = S // QB
TKOPS = 17


def _nkc(qb):
    return 4 * (qb + 1)


def _g0(qb):
    return 2 * qb * (qb + 1)


def build_program(debug=False, stop_after=99):
    nc = bass.Bass("TRN2", target_bir_lowering=False)

    x_fm = nc.dram_tensor("x_fm", [D, S], F32, kind="ExternalInput")
    x_own = nc.dram_tensor("x_own", [D, TOK], F32, kind="ExternalInput")
    wq = nc.dram_tensor("wq", [D, 4 * DH], F32, kind="ExternalInput")
    wk = nc.dram_tensor("wk", [D, DH], F32, kind="ExternalInput")
    wv = nc.dram_tensor("wv", [D, DH], F32, kind="ExternalInput")
    wo = nc.dram_tensor("wo", [D, D], F32, kind="ExternalInput")
    bo_t = nc.dram_tensor("bo_t", [128, NCH], F32, kind="ExternalInput")
    rwT = nc.dram_tensor("rwT", [D, 32], F32, kind="ExternalInput")
    ebias = nc.dram_tensor("ebias", [32, 1], F32, kind="ExternalInput")
    wg_all = nc.dram_tensor("wg_all", [NEX, D, IFF], F32, kind="ExternalInput")
    wu_all = nc.dram_tensor("wu_all", [NEX, D, IFF], F32, kind="ExternalInput")
    wd_all = nc.dram_tensor("wd_all", [NEX, IFF, D], F32, kind="ExternalInput")
    cs2 = nc.dram_tensor("cs2", [2, 128, S], F32, kind="ExternalInput")
    kaug = nc.dram_tensor("kaug", [5, S], BF16, kind="ExternalInput")
    qaug = nc.dram_tensor("qaug", [4, 5, S], BF16, kind="ExternalInput")
    dm_in = nc.dram_tensor("dm_in", [128, 4, QB], BF16, kind="ExternalInput")
    rotT = nc.dram_tensor("rotT", [DH, DH], BF16, kind="ExternalInput")
    idn_f_in = nc.dram_tensor("idn_f_in", [128, 128], F32, kind="ExternalInput")
    idn_b_in = nc.dram_tensor("idn_b_in", [DH, DH], BF16, kind="ExternalInput")
    ones_f_in = nc.dram_tensor("ones_f_in", [1, 128], F32, kind="ExternalInput")
    ones_b_in = nc.dram_tensor("ones_b_in", [128, 1], BF16, kind="ExternalInput")
    vones_in = nc.dram_tensor("vones_in", [128, 64 * 65], BF16, kind="ExternalInput")
    out = nc.dram_tensor("out", [D, TOK], F32, kind="ExternalOutput")

    cc_in = nc.dram_tensor("cc_in", [4 * DH, S], BF16, kind="Internal")
    cc_out = nc.dram_tensor("cc_out", [D, S], BF16, kind="Internal")
    probs_dram = nc.dram_tensor("probs_dram", [NEX, TOK], F32, kind="Internal")

    if debug:
        dbg_kaug = nc.dram_tensor("dbg_kaug", [69, S], BF16, kind="ExternalOutput")
        dbg_qaug = nc.dram_tensor("dbg_qaug", [4, 69, S], BF16, kind="ExternalOutput")
        dbg_vaug = nc.dram_tensor("dbg_vaug", [128, 64 * 65], BF16, kind="ExternalOutput")
        dbg_ctx = nc.dram_tensor("dbg_ctx", [4, DH, S], BF16, kind="ExternalOutput")
        dbg_x2 = nc.dram_tensor("dbg_x2", [128, NCH, TOK], F32, kind="ExternalOutput")
        dbg_ctxall = nc.dram_tensor("dbg_ctxall", [128, NCH, TOK], BF16, kind="ExternalOutput")
        dbg_flat = nc.dram_tensor("dbg_flat", [128, NCH, TOK], BF16, kind="ExternalOutput")
        dbg_probs = nc.dram_tensor("dbg_probs", [32, TOK], F32, kind="ExternalOutput")

    es = contextlib.ExitStack()

    def sb(name, shape, dt, stack=None, side=None):
        return (stack or es).enter_context(nc.sbuf_tensor(name, shape, dt, side=side))

    def psum(name, shape, dt, stack=None):
        return (stack or es).enter_context(nc.psum_tensor(name, shape, dt))

    def sem(name):
        return es.enter_context(nc.semaphore(name))

    dm_sb = sb("dm_sb", [128, 4, QB], BF16)
    idn_f = sb("idn_f", [128, 128], F32)
    idn_b = sb("idn_b", [DH, DH], BF16)
    rot_sb = sb("rot_sb", [DH, DH], BF16)
    on65 = sb("on65", [65, 128], F32)
    ones_b = sb("ones_b", [128, 1], BF16)

    esA = contextlib.ExitStack()
    q_aug = [sb(f"q_aug{h}", [69, S], BF16, stack=esA) for h in range(4)]
    k_aug = sb("k_aug", [69, S], BF16, stack=esA)
    v_aug = sb("v_aug", [128, 64 * 65], BF16, stack=esA)

    x_fm3 = x_fm.ap().rearrange("(ch p) s -> p ch s", p=128)
    cs3 = cs2.ap().rearrange("c p s -> p c s")

    es1t = contextlib.ExitStack()
    xt = [sb(f"xt{i}", [128, NCH, TT1], F32, stack=es1t, side="right") for i in range(2)]
    xb = [sb(f"xb{i}", [128, NCH, TT1], BF16, stack=es1t, side="right") for i in range(2)]
    sq = [sb(f"sq{i}", [128, NCH, TT1], BF16, stack=es1t, side="right") for i in range(2)]
    ln_t = [sb(f"ln_t{i}", [128, NCH, TT1], BF16, stack=es1t, side="right") for i in range(2)]
    tmp1 = sb("tmp1", [128, NCH, TT1], F32, stack=es1t, side="right")
    cs_t = [sb(f"cs_t{i}", [128, 2, TT1], F32, stack=es1t, side="right") for i in range(2)]
    wq_sb = sb("wq_sb", [128, NCH, 4 * DH], BF16, stack=es1t, side="right")
    wk_sb = sb("wk_sb", [128, NCH, DH], BF16, stack=es1t, side="right")
    wv_sb = sb("wv_sb", [128, NCH, DH], BF16, stack=es1t, side="right")
    k_bf = sb("k_bf", [DH, TT1], BF16, stack=es1t, side="right")
    v_bf = sb("v_bf", [DH, TT1], BF16, stack=es1t, side="right")
    q_bf = [sb(f"q_bf{i}", [DH, TT1], BF16, stack=es1t, side="right") for i in range(2)]
    t1b = sb("t1b", [DH, TT1], F32, stack=es1t, side="right")
    t2b = sb("t2b", [DH, TT1], F32, stack=es1t, side="right")
    rs_row = [sb(f"rs_row{i}", [1, 2 * TT1], F32, stack=es1t, side="right") for i in range(2)]
    fin_sc = sb("fin_sc", [1, 4 * TT1], F32, stack=es1t, side="right")
    SUM_ps = psum("SUM_ps", [1, TT1], F32, stack=es1t)
    SSQ_ps = psum("SSQ_ps", [1, TT1], F32, stack=es1t)
    RS_ps = psum("RS_ps", [128, 2 * TT1], F32, stack=es1t)
    KP2 = psum("KP2", [DH, 2, TT1], F32, stack=es1t)
    VPS = psum("VPS", [DH, TT1], F32, stack=es1t)
    VT = psum("VT", [128, DH], BF16, stack=es1t)
    QPS = psum("QPS", [DH, TT1], F32, stack=es1t)
    QROT = psum("QROT", [DH, TT1], F32, stack=es1t)

    s_xt = [sem("s_xt0"), sem("s_xt1")]
    s_xb = [sem("s_xb0"), sem("s_xb1")]
    s_sq = sem("s_sq")
    s_cs = [sem("s_cs0"), sem("s_cs1")]
    s_cs_done = sem("s_cs_done")
    s_pe_st = sem("s_pe_st")
    s_fin_ve = sem("s_fin_ve")
    s_act_sd = sem("s_act_sd")
    s_fin = sem("s_fin")
    s_pe_bc1 = sem("s_pe_bc1")
    s_norm = sem("s_norm")
    s_cst = sem("s_cst")
    s_cst_g = sem("s_cst_g")
    s_pe_k = sem("s_pe_k")
    s_pe_kr = sem("s_pe_kr")
    s_pe_v = sem("s_pe_v")
    s_pe_vt = sem("s_pe_vt")
    s_pe_q = sem("s_pe_q")
    s_pe_qr = sem("s_pe_qr")
    s_d_ke = sem("s_d_ke")
    s_d_kc = sem("s_d_kc")
    s_d_ve = sem("s_d_ve")
    s_d_vt = sem("s_d_vt")
    s_d_qe = sem("s_d_qe")
    s_d_qc = sem("s_d_qc")
    s_vch = sem("s_vch")

    VPT1 = 30

    def c_fin_ve(t):
        return VPT1 * t + 5

    def c_fin(t):
        return VPT1 * t + 6

    def c_norm(t):
        return VPT1 * t + 8

    def c_ke(t):
        return VPT1 * t + 9

    def c_kc(t):
        return VPT1 * t + 10

    def c_ve(t):
        return VPT1 * t + 13

    def c_vt(t):
        return VPT1 * t + 14

    def c_qe(qi):
        return VPT1 * (qi // 4) + 15 + 4 * (qi % 4)

    def c_qc(qi):
        return c_qe(qi) + 1

    def c_csd(t):
        return VPT1 * (t + 1)

    with nc.Block() as blk:
        @blk.sync
        def _(sy):
            sy.dma_start(out=dm_sb[:], in_=dm_in.ap()).then_inc(s_cst, 16)
            sy.dma_start(out=idn_f[:], in_=idn_f_in.ap()).then_inc(s_cst, 16)
            sy.dma_start(out=idn_b[:], in_=idn_b_in.ap()).then_inc(s_cst, 16)
            sy.dma_start(out=rot_sb[:], in_=rotT.ap()).then_inc(s_cst, 16)
            sy.dma_start(out=on65[0:1, :], in_=ones_f_in.ap()).then_inc(s_cst, 16)
            sy.dma_start(out=on65[64:65, :], in_=ones_f_in.ap()).then_inc(s_cst, 16)
            sy.dma_start(out=ones_b[:], in_=ones_b_in.ap()).then_inc(s_cst, 16)
            sy.dma_start(out=v_aug[:], in_=vones_in.ap()).then_inc(s_cst, 16)
            sy.dma_start(out=k_aug[64:69, :], in_=kaug.ap()).then_inc(s_cst, 16)
            for h in range(4):
                sy.dma_start(out=q_aug[h][64:69, :], in_=qaug.ap()[h]).then_inc(s_cst, 16)
            for t in range(NT1):
                if t >= 2:
                    sy.wait_ge(s_vch, c_norm(t - 2))
                sy.dma_start(out=xt[t % 2][:], in_=x_fm3[:, :, t * TT1:(t + 1) * TT1]).then_inc(s_xt[t % 2], 16)
                if t >= 2:
                    sy.wait_ge(s_vch, c_csd(t - 2))
                sy.dma_start(out=cs_t[t % 2][:], in_=cs3[:, :, t * TT1:(t + 1) * TT1]).then_inc(s_cs[t % 2], 16)

        @blk.gpsimd
        def _(g):
            g.dma_start(out=wq_sb[:], in_=wq.ap().rearrange("(ch p) f -> p ch f", p=128)).then_inc(s_cst_g, 16)
            g.dma_start(out=wk_sb[:], in_=wk.ap().rearrange("(ch p) f -> p ch f", p=128)).then_inc(s_cst_g, 16)
            g.dma_start(out=wv_sb[:], in_=wv.ap().rearrange("(ch p) f -> p ch f", p=128)).then_inc(s_cst_g, 16)
            for t in range(NT1):
                if t >= 2:
                    g.wait_ge(s_pe_st, t - 1)
                g.dma_start(out=xb[t % 2][:], in_=x_fm3[:, :, t * TT1:(t + 1) * TT1]).then_inc(s_xb[t % 2], 16)

        @blk.scalar
        def _(sc):
            for t in range(NT1):
                sc.wait_ge(s_xt[t % 2], 16 * (t // 2 + 1))
                if t >= 2:
                    sc.wait_ge(s_pe_st, t - 1)
                sc.activation(out=sq[t % 2][:], in_=xt[t % 2][:], func=AF.Square).then_inc(s_sq, 1)
                sc.wait_ge(s_vch, c_fin_ve(t))
                sc.activation(out=fin_sc[0:1, 3 * TT1:4 * TT1], in_=fin_sc[0:1, 2 * TT1:3 * TT1],
                              func=AF.Sqrt).then_inc(s_act_sd, 1)

        @blk.tensor
        def _(te):
            te.wait_ge(s_cst, 16 * 13)
            te.wait_ge(s_cst_g, 48)
            for t in range(NT1):
                par = t % 2
                te.wait_ge(s_xb[t % 2], 16 * (t // 2 + 1))
                te.wait_ge(s_sq, t + 1)
                if t >= 1:
                    te.wait_ge(s_vch, c_fin(t - 1))
                for ch in range(NCH):
                    te.matmul(SUM_ps[:], ones_b[:], xb[par][:, ch, :],
                              start=(ch == 0), stop=(ch == NCH - 1))
                for ch in range(NCH):
                    mm = te.matmul(SSQ_ps[:], ones_b[:], sq[par][:, ch, :],
                                   start=(ch == 0), stop=(ch == NCH - 1))
                mm.then_inc(s_pe_st, 1)
                te.wait_ge(s_vch, c_fin(t))
                if t >= 1:
                    te.wait_ge(s_vch, c_norm(t - 1))
                te.matmul(RS_ps[:], on65[0:1, :], rs_row[par][:],
                          start=True, stop=True).then_inc(s_pe_bc1, 1)
                te.wait_ge(s_vch, c_norm(t))
                if t >= 1:
                    te.wait_ge(s_vch, c_kc(t - 1))
                for ch in range(NCH):
                    mm = te.matmul(KP2[:, 0, :], wk_sb[:, ch, :], ln_t[par][:, ch, :],
                                   start=(ch == 0), stop=(ch == NCH - 1))
                mm.then_inc(s_pe_k, 1)
                te.wait_ge(s_vch, c_ke(t))
                te.matmul(KP2[:, 1, :], rot_sb[:], k_bf[:], start=True, stop=True).then_inc(s_pe_kr, 1)
                if t >= 1:
                    te.wait_ge(s_vch, c_ve(t - 1))
                for ch in range(NCH):
                    mm = te.matmul(VPS[:], wv_sb[:, ch, :], ln_t[par][:, ch, :],
                                   start=(ch == 0), stop=(ch == NCH - 1))
                mm.then_inc(s_pe_v, 1)
                if t >= 1:
                    te.wait_ge(s_vch, c_vt(t - 1))
                te.wait_ge(s_vch, c_ve(t))
                te.transpose(VT[:], v_bf[:], idn_b[:]).then_inc(s_pe_vt, 1)
                for h in range(4):
                    qi = 4 * t + h
                    if qi >= 1:
                        te.wait_ge(s_vch, c_qe(qi - 1))
                    for ch in range(NCH):
                        mm = te.matmul(QPS[:], wq_sb[:, ch, h * DH:(h + 1) * DH],
                                       ln_t[par][:, ch, :],
                                       start=(ch == 0), stop=(ch == NCH - 1))
                    mm.then_inc(s_pe_q, 1)
                    if qi >= 1:
                        te.wait_ge(s_vch, c_qc(qi - 1))
                    te.wait_ge(s_vch, c_qe(qi))
                    te.matmul(QROT[:], rot_sb[:], q_bf[qi % 2][:],
                              start=True, stop=True).then_inc(s_pe_qr, 1)

        @blk.vector
        def _(v):
            vch = [0]

            def chain(op_fn, *incs):
                if vch[0] > 0:
                    v.wait_ge(s_vch, vch[0])
                ins = op_fn()
                ins.then_inc(s_vch, 1)
                for sm, n in incs:
                    ins.then_inc(sm, n)
                vch[0] += 1

            for t in range(NT1):
                par = t % 2
                T0 = t * TT1
                v.wait_ge(s_pe_st, t + 1)
                if t >= 2:
                    v.wait_ge(s_pe_bc1, t - 1)
                chain(lambda par=par: v.tensor_scalar(out=rs_row[par][0:1, 0:TT1], in0=SUM_ps[:],
                                scalar1=-1.0 / D, scalar2=None, op0=ALU.mult))
                chain(lambda: v.tensor_scalar(out=fin_sc[0:1, 0:TT1], in0=SSQ_ps[:],
                                scalar1=1.0 / D, scalar2=None, op0=ALU.mult))
                chain(lambda par=par: v.tensor_tensor(out=fin_sc[0:1, TT1:2 * TT1], in0=rs_row[par][0:1, 0:TT1],
                                in1=rs_row[par][0:1, 0:TT1], op=ALU.mult))
                chain(lambda: v.tensor_tensor(out=fin_sc[0:1, 2 * TT1:3 * TT1], in0=fin_sc[0:1, 0:TT1],
                                in1=fin_sc[0:1, TT1:2 * TT1], op=ALU.subtract))
                chain(lambda: v.tensor_scalar(out=fin_sc[0:1, 2 * TT1:3 * TT1], in0=fin_sc[0:1, 2 * TT1:3 * TT1],
                                scalar1=EPS, scalar2=None, op0=ALU.add))
                v.wait_ge(s_act_sd, t + 1)
                chain(lambda par=par: v.reciprocal(rs_row[par][0:1, TT1:2 * TT1], fin_sc[0:1, 3 * TT1:4 * TT1]))
                v.wait_ge(s_pe_bc1, t + 1)
                if t >= 2:
                    v.wait_ge(s_pe_q, 4 * (t - 2) + 4)
                negmu_b = bass.AP(RS_ps, 0, [[2 * TT1, 128], [0, NCH], [1, TT1]])
                r_b = bass.AP(RS_ps, TT1, [[2 * TT1, 128], [0, NCH], [1, TT1]])
                chain(lambda par=par, negmu_b=negmu_b: v.tensor_tensor(out=tmp1[:], in0=xt[par][:], in1=negmu_b, op=ALU.add))
                chain(lambda par=par, r_b=r_b: v.tensor_tensor(out=ln_t[par][:], in0=tmp1[:], in1=r_b, op=ALU.mult))
                v.wait_ge(s_pe_k, t + 1)
                if t >= 1:
                    v.wait_ge(s_pe_kr, t)
                chain(lambda: v.tensor_copy(k_bf[:], KP2[:, 0, :]))
                v.wait_ge(s_pe_kr, t + 1)
                v.wait_ge(s_cs[t % 2], 16 * (t // 2 + 1))
                chain(lambda par=par: v.tensor_tensor(out=t1b[:], in0=KP2[:, 1, :], in1=cs_t[par][0:DH, 1, :],
                                op=ALU.mult))
                chain(lambda par=par: v.tensor_tensor(out=t2b[:], in0=k_bf[:], in1=cs_t[par][0:DH, 0, :], op=ALU.mult))
                chain(lambda T0=T0: v.tensor_tensor(out=k_aug[0:DH, T0:T0 + TT1], in0=t1b[:], in1=t2b[:], op=ALU.add))
                v.wait_ge(s_pe_v, t + 1)
                if t >= 1:
                    v.wait_ge(s_pe_vt, t)
                chain(lambda: v.tensor_copy(v_bf[:], VPS[:]))
                v.wait_ge(s_pe_vt, t + 1)
                chain(lambda t=t: v.tensor_copy(v_aug[:, t * 65:t * 65 + 64], VT[:]))
                for h in range(4):
                    qi = 4 * t + h
                    v.wait_ge(s_pe_q, qi + 1)
                    if qi >= 2:
                        v.wait_ge(s_pe_qr, qi - 1)
                    chain(lambda qi=qi: v.tensor_copy(q_bf[qi % 2][:], QPS[:]))
                    v.wait_ge(s_pe_qr, qi + 1)
                    chain(lambda par=par: v.tensor_tensor(out=t1b[:], in0=QROT[:], in1=cs_t[par][0:DH, 1, :],
                                    op=ALU.mult))
                    chain(lambda par=par, qi=qi: v.tensor_tensor(out=t2b[:], in0=q_bf[qi % 2][:], in1=cs_t[par][0:DH, 0, :],
                                    op=ALU.mult))
                    chain(lambda h=h, T0=T0: v.tensor_tensor(out=q_aug[h][0:DH, T0:T0 + TT1], in0=t1b[:], in1=t2b[:],
                                         op=ALU.add))

    es1t.close()

    if debug:
        s_dbg1 = sem("s_dbg1")
        with nc.Block() as blk:
            @blk.sync
            def _(sy):
                sy.dma_start(out=dbg_kaug.ap(), in_=k_aug[:, :]).then_inc(s_dbg1, 16)
                for h in range(4):
                    sy.dma_start(out=dbg_qaug.ap()[h], in_=q_aug[h][:, :]).then_inc(s_dbg1, 16)
                sy.dma_start(out=dbg_vaug.ap(), in_=v_aug[:, :]).then_inc(s_dbg1, 16)
                sy.wait_ge(s_dbg1, 16 * 6)


    if stop_after < 2:
        return nc

    es23 = contextlib.ExitStack()
    ctx_sb = [sb(f"ctx{h}", [DH, S], BF16, stack=es23, side="right") for h in range(4)]

    es2 = contextlib.ExitStack()
    ts_sb = sb("ts_sb", [128, 4 * QB], F32, stack=es2, side="right")
    pt_sb = [sb(f"pt{i}", [128, 4 * QB], BF16, stack=es2, side="right") for i in range(2)]
    den_sb = sb("den_sb", [65, QB], F32, stack=es2, side="right")
    rb_sb = sb("rb_sb", [DH, QB], F32, stack=es2, side="right")
    ST = psum("ST", [128, 4 * QB], F32, stack=es2)
    CTX = psum("CTX", [65, 4 * QB], F32, stack=es2)

    s_qk = sem("s_qk")
    s_tanh = sem("s_tanh")
    s_exp = sem("s_exp")
    s_mask = sem("s_mask")
    s_pv = sem("s_pv")
    s_den = sem("s_den")
    s_pe_bc2 = sem("s_pe_bc2")
    s_nrm = sem("s_nrm")
    s_vch2 = sem("s_vch2")

    def _emit_pv(te, qb, kc, G, nkc):
        diag_j = kc - 4 * qb
        if diag_j >= 0:
            te.wait_ge(s_mask, 4 * qb + diag_j + 1)
        else:
            te.wait_ge(s_exp, G + 1)
        if kc == 0 and qb > 0:
            te.wait_ge(s_nrm, 4 * qb)
        par = G % 2
        for h in range(4):
            mm = te.matmul(CTX[:, h * QB:(h + 1) * QB],
                           v_aug[:, kc * 65:kc * 65 + 65],
                           pt_sb[par][:, h * QB:(h + 1) * QB],
                           start=(kc == 0), stop=(kc == nkc - 1))
        mm.then_inc(s_pv, 1)

    with nc.Block() as blk:
        @blk.tensor
        def _(te):
            for qb in range(NQB):
                Q0 = qb * QB
                nkc = _nkc(qb)
                g0 = _g0(qb)
                for kc in range(nkc):
                    G = g0 + kc
                    if G > 0:
                        te.wait_ge(s_tanh, G)
                    if kc == 0 and qb > 0:
                        te.wait_ge(s_nrm, 4 * qb)
                    for h in range(4):
                        mm = te.matmul(ST[:, h * QB:(h + 1) * QB],
                                       k_aug[:, kc * 128:(kc + 1) * 128],
                                       q_aug[h][:, Q0:Q0 + QB], start=True, stop=True)
                    mm.then_inc(s_qk, 1)
                    if kc > 0:
                        _emit_pv(te, qb, kc - 1, G - 1, nkc)
                _emit_pv(te, qb, nkc - 1, g0 + nkc - 1, nkc)
                for h in range(4):
                    te.wait_ge(s_vch2, 8 * qb + 2 * h + 2)
                    te.matmul(ST[0:DH, h * QB:(h + 1) * QB], on65[64:65, 0:DH],
                              den_sb[64:65, :], start=True, stop=True).then_inc(s_pe_bc2, 1)

        @blk.scalar
        def _(sc):
            for qb in range(NQB):
                nkc = _nkc(qb)
                g0 = _g0(qb)
                for kc in range(nkc):
                    G = g0 + kc
                    sc.wait_ge(s_qk, G + 1)
                    sc.activation(out=ts_sb[:], in_=ST[:], func=AF.Tanh,
                                  scale=1.0 / 240.0).then_inc(s_tanh, 1)
                    if G >= 2:
                        sc.wait_ge(s_pv, G - 1)
                    sc.activation(out=pt_sb[G % 2][:], in_=ts_sb[:], func=AF.Exp,
                                  scale=30.0).then_inc(s_exp, 1)

        @blk.vector
        def _(v):
            for qb in range(NQB):
                Q0 = qb * QB
                nkc = _nkc(qb)
                g0 = _g0(qb)
                for j in range(4):
                    G = g0 + 4 * qb + j
                    v.wait_ge(s_exp, G + 1)
                    par = G % 2
                    pt3 = bass.AP(pt_sb[par], 0, [[4 * QB, 128], [QB, 4], [1, QB]])
                    dmj = bass.AP(dm_sb, j * QB, [[4 * QB, 128], [0, 4], [1, QB]])
                    v.tensor_tensor(out=pt3, in0=pt3, in1=dmj, op=ALU.mult).then_inc(s_mask, 1)
                v.wait_ge(s_pv, g0 + nkc)
                for h in range(4):
                    v.tensor_copy(den_sb[64:65, :], CTX[64:65, h * QB:(h + 1) * QB]).then_inc(s_vch2, 1)
                    v.wait_ge(s_vch2, 8 * qb + 2 * h + 1)
                    v.reciprocal(den_sb[64:65, :], den_sb[64:65, :]).then_inc(s_vch2, 1)
                    v.wait_ge(s_pe_bc2, 4 * qb + h + 1)
                    v.tensor_copy(rb_sb[:], ST[0:DH, h * QB:(h + 1) * QB])
                    v.tensor_tensor(out=ctx_sb[h][:, Q0:Q0 + QB],
                                    in0=CTX[0:DH, h * QB:(h + 1) * QB],
                                    in1=rb_sb[:], op=ALU.mult).then_inc(s_nrm, 1)

    es2.close()


    if stop_after < 3:
        return nc

    s_cc1 = sem("s_cc1")
    s_cc2 = sem("s_cc2")
    with nc.Block() as blk:
        @blk.gpsimd
        def _(g):
            for h in range(4):
                g.dma_start(out=cc_in.ap()[h * DH:(h + 1) * DH, :], in_=ctx_sb[h][:, :]).then_inc(s_cc1, 16)
            g.wait_ge(s_cc1, 64)
            g.collective_compute(
                "AllGather", ALU.bypass,
                replica_groups=[list(range(NCORES))],
                ins=[cc_in.ap()], outs=[cc_out.ap()],
            ).then_inc(s_cc2, 1)
            g.wait_ge(s_cc2, 1)

    if debug:
        s_dbg2 = sem("s_dbg2")
        with nc.Block() as blk:
            @blk.sync
            def _(sy):
                for h in range(4):
                    sy.dma_start(out=dbg_ctx.ap()[h], in_=ctx_sb[h][:, :]).then_inc(s_dbg2, 16)
                sy.wait_ge(s_dbg2, 64)

    es23.close()
    esA.close()


    if stop_after < 4:
        return nc

    es4 = contextlib.ExitStack()
    x2 = sb("x2", [128, NCH, TOK], F32, stack=es4, side="right")
    flat = sb("flat", [128, NCH, TOK], BF16, stack=es4, side="right")

    es4a1 = contextlib.ExitStack()
    ctx_all = sb("ctx_all", [128, NCH, TOK], BF16, stack=es4a1, side="right")
    es4a2 = contextlib.ExitStack()
    wo_c = [sb(f"wo_c{i}", [128, NCH, 128], BF16, stack=es4a2, side="right") for i in range(2)]
    bo_sb = sb("bo_sb", [128, NCH], F32, stack=es4a2, side="right")
    ot_sb = sb("ot_sb", [128, QB], F32, stack=es4a2, side="right")
    OPS = psum("OPS", [128, 2, TOK], F32, stack=es4a2)

    s_ctxa = sem("s_ctxa")
    s_xo = sem("s_xo")
    s_wo = [sem("s_wo0"), sem("s_wo1")]
    s_pe_o = sem("s_pe_o")
    s_d_o = sem("s_d_o")
    s_b4c = sem("s_b4c")
    s_xb2 = sem("s_xb2")

    cc_out3 = cc_out.ap().rearrange("(ch p) s -> p ch s", p=128)
    wo3 = wo.ap().rearrange("(ch p) d -> p ch d", p=128)

    with nc.Block() as blk:
        @blk.sync
        def _(sy):
            pid = sy.snap(sy.partition_id(), min_val=0, max_val=NCORES - 1)
            sy.dma_start(out=ctx_all[:],
                         in_=cc_out3[:, :, bass.ds(pid * TOK, TOK)]).then_inc(s_ctxa, 16)
            sy.dma_start(out=x2[:], in_=x_own.ap().rearrange("(ch p) t -> p ch t", p=128)).then_inc(s_xo, 16)
            sy.dma_start(out=bo_sb[:], in_=bo_t.ap()).then_inc(s_b4c, 16)
            if debug:
                sy.wait_ge(s_ctxa, 16)
                sy.dma_start(out=dbg_ctxall.ap(), in_=ctx_all[:]).then_inc(s_b4c, 16)

        @blk.gpsimd
        def _(g):
            for dc in range(NCH):
                if dc >= 2:
                    g.wait_ge(s_pe_o, 2 * (dc - 1))
                g.dma_start(out=wo_c[dc % 2][:], in_=wo3[:, :, dc * 128:(dc + 1) * 128]).then_inc(s_wo[dc % 2], 16)

        @blk.tensor
        def _(te):
            te.wait_ge(s_ctxa, 16)
            for dc in range(NCH):
                te.wait_ge(s_wo[dc % 2], 16 * (dc // 2 + 1))
                if dc >= 2:
                    te.wait_ge(s_d_o, 2 * (dc - 1))
                for tb in range(2):
                    for ch in range(NCH):
                        mm = te.matmul(OPS[:, dc % 2, tb * QB:(tb + 1) * QB],
                                       wo_c[dc % 2][:, ch, :],
                                       ctx_all[:, ch, tb * QB:(tb + 1) * QB],
                                       start=(ch == 0), stop=(ch == NCH - 1))
                    mm.then_inc(s_pe_o, 1)

        @blk.vector
        def _(v):
            v.wait_ge(s_xo, 16)
            v.wait_ge(s_b4c, 16)
            for dc in range(NCH):
                for tb in range(2):
                    v.wait_ge(s_pe_o, 2 * dc + tb + 1)
                    v.tensor_scalar(out=ot_sb[:], in0=OPS[:, dc % 2, tb * QB:(tb + 1) * QB],
                                    scalar1=bo_sb[:, dc:dc + 1], scalar2=None, op0=ALU.add)
                    v.tensor_tensor(out=x2[:, dc, tb * QB:(tb + 1) * QB],
                                    in0=x2[:, dc, tb * QB:(tb + 1) * QB],
                                    in1=ot_sb[:], op=ALU.add).then_inc(s_d_o, 1)
            v.tensor_copy(ctx_all[:], x2[:]).then_inc(s_xb2, 1)

    es4a2.close()


    if stop_after < 5:
        return nc

    es4b = contextlib.ExitStack()
    sq2 = sb("sq2", [128, NCH, TOK], BF16, stack=es4b, side="right")
    tmp4 = sb("tmp4", [128, TOK], F32, stack=es4b, side="right")
    rs2_row = sb("rs2_row", [1, 2 * TOK], F32, stack=es4b, side="right")
    fin2 = sb("fin2", [1, 4 * TOK], F32, stack=es4b, side="right")
    SUM2 = psum("SUM2", [1, TOK], F32, stack=es4b)
    SSQ2 = psum("SSQ2", [1, TOK], F32, stack=es4b)
    RS2 = psum("RS2", [128, 2 * TOK], F32, stack=es4b)

    s_sq2 = sem("s_sq2")
    s_pe_st2 = sem("s_pe_st2")
    s_fin2_ve = sem("s_fin2_ve")
    s_act_sd2 = sem("s_act_sd2")
    s_fin2 = sem("s_fin2")
    s_pe_bc4 = sem("s_pe_bc4")
    s_norm2 = sem("s_norm2")
    s_vch4 = sem("s_vch4")

    with nc.Block() as blk:
        @blk.scalar
        def _(sc):
            sc.activation(out=sq2[:], in_=x2[:], func=AF.Square).then_inc(s_sq2, 1)
            sc.wait_ge(s_fin2_ve, 1)
            sc.activation(out=fin2[0:1, 3 * TOK:4 * TOK], in_=fin2[0:1, 2 * TOK:3 * TOK],
                          func=AF.Sqrt).then_inc(s_act_sd2, 1)

        @blk.tensor
        def _(te):
            for tb in range(2):
                for ch in range(NCH):
                    mm = te.matmul(SUM2[0:1, tb * QB:(tb + 1) * QB], ones_b[:],
                                   ctx_all[:, ch, tb * QB:(tb + 1) * QB],
                                   start=(ch == 0), stop=(ch == NCH - 1))
            mm.then_inc(s_pe_st2, 1)
            te.wait_ge(s_sq2, 1)
            for tb in range(2):
                for ch in range(NCH):
                    mm = te.matmul(SSQ2[0:1, tb * QB:(tb + 1) * QB], ones_b[:],
                                   sq2[:, ch, tb * QB:(tb + 1) * QB],
                                   start=(ch == 0), stop=(ch == NCH - 1))
            mm.then_inc(s_pe_st2, 1)
            te.wait_ge(s_fin2, 1)
            for j in range(4):
                te.matmul(RS2[:, j * QB:(j + 1) * QB], on65[0:1, :],
                          rs2_row[0:1, j * QB:(j + 1) * QB],
                          start=True, stop=True).then_inc(s_pe_bc4, 1)

        @blk.vector
        def _(v):
            v.wait_ge(s_pe_st2, 2)
            v.tensor_scalar(out=rs2_row[0:1, 0:TOK], in0=SUM2[:], scalar1=-1.0 / D,
                            scalar2=None, op0=ALU.mult).then_inc(s_vch4, 1)
            v.tensor_scalar(out=fin2[0:1, 0:TOK], in0=SSQ2[:], scalar1=1.0 / D,
                            scalar2=None, op0=ALU.mult).then_inc(s_vch4, 1)
            v.wait_ge(s_vch4, 2)
            v.tensor_tensor(out=fin2[0:1, TOK:2 * TOK], in0=rs2_row[0:1, 0:TOK],
                            in1=rs2_row[0:1, 0:TOK], op=ALU.mult).then_inc(s_vch4, 1)
            v.wait_ge(s_vch4, 3)
            v.tensor_tensor(out=fin2[0:1, 2 * TOK:3 * TOK], in0=fin2[0:1, 0:TOK],
                            in1=fin2[0:1, TOK:2 * TOK], op=ALU.subtract).then_inc(s_vch4, 1)
            v.wait_ge(s_vch4, 4)
            v.tensor_scalar(out=fin2[0:1, 2 * TOK:3 * TOK], in0=fin2[0:1, 2 * TOK:3 * TOK],
                            scalar1=EPS, scalar2=None, op0=ALU.add).then_inc(s_fin2_ve, 1)
            v.wait_ge(s_act_sd2, 1)
            v.reciprocal(rs2_row[0:1, TOK:2 * TOK], fin2[0:1, 3 * TOK:4 * TOK]).then_inc(s_fin2, 1)
            v.wait_ge(s_pe_bc4, 4)
            for ch in range(NCH):
                v.tensor_tensor(out=tmp4[:], in0=x2[:, ch, :], in1=RS2[:, 0:TOK], op=ALU.add)
                v.tensor_tensor(out=flat[:, ch, :], in0=tmp4[:], in1=RS2[:, TOK:2 * TOK],
                                op=ALU.mult).then_inc(s_norm2, 1)

    es4b.close()
    es4a1.close()


    if stop_after < 6:
        return nc

    es4c = contextlib.ExitStack()
    rw_sb = sb("rw_sb", [128, NCH, 32], BF16, stack=es4c, side="right")
    eb_sb = sb("eb_sb", [32, 1], F32, stack=es4c, side="right")
    sp_a = sb("sp_a", [32, TOK], F32, stack=es4c, side="right")
    sp_r = sb("sp_r", [32, TOK], F32, stack=es4c, side="right")
    sc_sb = sb("sc_sb", [32, TOK], F32, stack=es4c, side="right")
    y_sb = sb("y_sb", [32, TOK], F32, stack=es4c, side="right")
    sc_t = sb("sc_t", [128, 8, 32], F32, stack=es4c, side="right")
    y_t = sb("y_t", [128, 8, 32], F32, stack=es4c, side="right")
    yc_t = sb("yc_t", [128, 8, 32], F32, stack=es4c, side="right")
    et_t = sb("et_t", [128, 32], F32, stack=es4c, side="right")
    mx_t = sb("mx_t", [128, 2], F32, stack=es4c, side="right")
    pr_t = sb("pr_t", [128, 8, 32], F32, stack=es4c, side="right")
    prb = sb("prb", [32, TOK], F32, stack=es4c, side="right")
    ones_row = sb("ones_row", [1, TOK], F32, stack=es4c, side="right")
    RPS = psum("RPS", [32, TOK], F32, stack=es4c)
    TPSa = psum("TPSa", [128, 32], F32, stack=es4c)
    TPSb = psum("TPSb", [128, 32], F32, stack=es4c)
    TPS2 = psum("TPS2", [32, 128], F32, stack=es4c)

    s_b4c2 = sem("s_b4c2")
    s_pe_r = sem("s_pe_r")
    s_sp1 = sem("s_sp1")
    s_spv = sem("s_spv")
    s_scs = sem("s_scs")
    s_yv = sem("s_yv")
    s_pe_t1 = sem("s_pe_t1")
    s_d_t1 = sem("s_d_t1")
    s_tk = sem("s_tk")
    s_pe_t2 = sem("s_pe_t2")
    s_d_t2 = sem("s_d_t2")
    s_prd = sem("s_prd")

    with nc.Block() as blk:
        @blk.gpsimd
        def _(g):
            g.dma_start(out=rw_sb[:], in_=rwT.ap().rearrange("(ch p) e -> p ch e", p=128)).then_inc(s_b4c2, 16)
            g.dma_start(out=eb_sb[:], in_=ebias.ap()).then_inc(s_b4c2, 16)

        @blk.tensor
        def _(te):
            te.wait_ge(s_b4c2, 32)
            for tb in range(2):
                for ch in range(NCH):
                    mm = te.matmul(RPS[:, tb * QB:(tb + 1) * QB], rw_sb[:, ch, :],
                                   flat[:, ch, tb * QB:(tb + 1) * QB],
                                   start=(ch == 0), stop=(ch == NCH - 1))
                mm.then_inc(s_pe_r, 1)
            te.wait_ge(s_yv, 1)
            for b in range(8):
                if b >= 1:
                    te.wait_ge(s_d_t1, b)
                te.transpose(TPSa[:], sc_sb[:, b * 128:(b + 1) * 128], idn_f[0:32, 0:32])
                te.transpose(TPSb[:], y_sb[:, b * 128:(b + 1) * 128],
                             idn_f[0:32, 0:32]).then_inc(s_pe_t1, 1)
            for b in range(8):
                te.wait_ge(s_tk, TKOPS * (b + 1))
                if b >= 1:
                    te.wait_ge(s_d_t2, b)
                te.transpose(TPS2[:], pr_t[:, b, :], idn_f[:, :]).then_inc(s_pe_t2, 1)

        @blk.scalar
        def _(sc):
            sc.wait_ge(s_pe_r, 2)
            sc.activation(out=sp_a[:], in_=RPS[:], func=AF.Abs)
            sc.activation(out=sp_a[:], in_=sp_a[:], func=AF.Exp, scale=-1.0)
            sc.activation(out=sp_a[:], in_=sp_a[:], func=AF.Ln, bias=1.0)
            sc.activation(out=sp_r[:], in_=RPS[:], func=AF.Relu).then_inc(s_sp1, 1)
            sc.wait_ge(s_spv, 1)
            sc.activation(out=sc_sb[:], in_=sc_sb[:], func=AF.Sqrt).then_inc(s_scs, 1)

        @blk.vector
        def _(v):
            v.wait_ge(s_sp1, 1)
            v.tensor_tensor(out=sc_sb[:], in0=sp_a[:], in1=sp_r[:], op=ALU.add).then_inc(s_spv, 1)
            v.wait_ge(s_scs, 1)
            v.wait_ge(s_b4c2, 32)
            v.tensor_scalar(out=y_sb[:], in0=sc_sb[:], scalar1=eb_sb[:, 0:1], scalar2=None,
                            op0=ALU.add).then_inc(s_yv, 1)
            for b in range(8):
                v.wait_ge(s_pe_t1, b + 1)
                v.tensor_copy(sc_t[:, b, :], TPSa[:])
                v.tensor_copy(y_t[:, b, :], TPSb[:]).then_inc(s_d_t1, 1)
            tk = [0]

            def step(op_fn):
                if tk[0] > 0:
                    v.wait_ge(s_tk, tk[0])
                op_fn().then_inc(s_tk, 1)
                tk[0] += 1

            for b in range(8):
                yb = y_t[:, b, :]
                step(lambda b=b, yb=yb: v.tensor_copy(yc_t[:, b, :], yb))
                for it in range(4):
                    dst = mx_t[:, 0:1] if it < 3 else mx_t[:, 1:2]
                    step(lambda dst=dst, yb=yb: v.tensor_reduce(dst, yb, axis=AX.X, op=ALU.max))
                    if it < 3:
                        step(lambda yb=yb: v.tensor_scalar(out=et_t[:], in0=yb, scalar1=mx_t[:, 0:1],
                                                           scalar2=None, op0=ALU.is_equal))
                        step(lambda yb=yb: v.scalar_tensor_tensor(out=yb, in0=et_t[:], scalar=-1e30,
                                                                  in1=yb, op0=ALU.mult, op1=ALU.add))
                step(lambda b=b: v.tensor_scalar(out=et_t[:], in0=yc_t[:, b, :], scalar1=mx_t[:, 1:2],
                                                 scalar2=None, op0=ALU.is_ge))
                step(lambda b=b: v.tensor_tensor(out=pr_t[:, b, :], in0=et_t[:], in1=sc_t[:, b, :],
                                                 op=ALU.mult))
                step(lambda b=b: v.tensor_reduce(mx_t[:, 0:1], pr_t[:, b, :], axis=AX.X, op=ALU.add))
                step(lambda: v.tensor_scalar(out=mx_t[:, 0:1], in0=mx_t[:, 0:1], scalar1=1e-20,
                                             scalar2=None, op0=ALU.add))
                step(lambda: v.reciprocal(mx_t[:, 0:1], mx_t[:, 0:1]))
                step(lambda b=b: v.tensor_scalar(out=pr_t[:, b, :], in0=pr_t[:, b, :],
                                                 scalar1=mx_t[:, 0:1], scalar2=None, op0=ALU.mult))
            assert tk[0] == TKOPS * 8, tk[0]
            for b in range(8):
                v.wait_ge(s_pe_t2, b + 1)
                v.tensor_copy(prb[:, b * 128:(b + 1) * 128], TPS2[:]).then_inc(s_d_t2, 1)
            v.memset(ones_row[:], 1.0).then_inc(s_d_t2, 1)

        @blk.sync
        def _(sy):
            sy.wait_ge(s_d_t2, 9)
            sy.dma_start(out=probs_dram.ap()[0:32, :], in_=prb[:, :]).then_inc(s_prd, 16)
            sy.dma_start(out=probs_dram.ap()[32:33, :], in_=ones_row[:, :]).then_inc(s_prd, 16)
            sy.wait_ge(s_prd, 32)

    es4c.close()

    if debug:
        s_dbg4 = sem("s_dbg4")
        with nc.Block() as blk:
            @blk.sync
            def _(sy):
                sy.dma_start(out=dbg_x2.ap(), in_=x2[:]).then_inc(s_dbg4, 16)
                sy.dma_start(out=dbg_flat.ap(), in_=flat[:]).then_inc(s_dbg4, 16)
                sy.dma_start(out=dbg_probs.ap(), in_=probs_dram.ap()[0:32, :]).then_inc(s_dbg4, 16)
                sy.wait_ge(s_dbg4, 48)


    if stop_after < 7:
        return nc

    es5 = contextlib.ExitStack()
    h_sb = [sb(f"h_sb{i}", [128, 8, TOK], BF16, stack=es5, side="right") for i in range(2)]
    wgc = [sb(f"wgc{i}", [128, NCH, 128], BF16, stack=es5, side="right") for i in range(2)]
    wuc = [sb(f"wuc{i}", [128, NCH, 128], BF16, stack=es5, side="right") for i in range(2)]
    wdc = [sb(f"wdc{i}", [128, 8, 128], BF16, stack=es5, side="right") for i in range(2)]
    gc_b = [sb(f"gc_b{i}", [128, TOK], F32, stack=es5, side="right") for i in range(2)]
    sg_b = [sb(f"sg_b{i}", [128, TOK], F32, stack=es5, side="right") for i in range(2)]
    uc_b = sb("uc_b", [128, TOK], F32, stack=es5, side="right")
    tt_b = sb("tt_b", [128, TOK], F32, stack=es5, side="right")
    p_bc = [sb(f"p_bc{i}", [128, TOK], F32, stack=es5, side="right") for i in range(2)]
    GPS = psum("GPS", [128, TOK], F32, stack=es5)
    UPS = psum("UPS", [128, TOK], F32, stack=es5)
    DPS = psum("DPS", [128, 2, TOK], F32, stack=es5)

    s_wg = [sem("s_wg0"), sem("s_wg1")]
    s_wu = [sem("s_wu0"), sem("s_wu1")]
    s_wd = [sem("s_wd0"), sem("s_wd1")]
    s_pbc = [sem("s_pbc0"), sem("s_pbc1")]
    s_peg = sem("s_peg")
    s_peu = sem("s_peu")
    s_ped = sem("s_ped")
    s_gc = sem("s_gc")
    s_uc = sem("s_uc")
    s_sg = sem("s_sg")
    s_h = sem("s_h")
    s_x2o = sem("s_x2o")
    s_out = sem("s_out")

    NDC = NEX * 16

    with nc.Block() as blk:
        @blk.gpsimd
        def _(g):
            for e in range(NEX + 1):
                if e < NEX:
                    wg3 = wg_all.ap()[e].rearrange("(ch p) f -> p ch f", p=128)
                    wu3 = wu_all.ap()[e].rearrange("(ch p) f -> p ch f", p=128)
                    for fi in range(8):
                        gidx = e * 8 + fi
                        if gidx >= 2:
                            g.wait_ge(s_peg, gidx - 1)
                        g.dma_start(out=wgc[gidx % 2][:], in_=wg3[:, :, fi * 128:(fi + 1) * 128]).then_inc(s_wg[gidx % 2], 16)
                        if gidx >= 2:
                            g.wait_ge(s_peu, gidx - 1)
                        g.dma_start(out=wuc[gidx % 2][:], in_=wu3[:, :, fi * 128:(fi + 1) * 128]).then_inc(s_wu[gidx % 2], 16)
                if e >= 1:
                    wd3 = wd_all.ap()[e - 1].rearrange("(f p) d -> p f d", p=128)
                    for dc in range(16):
                        didx = (e - 1) * 16 + dc
                        if didx >= 2:
                            g.wait_ge(s_ped, didx - 1)
                        g.dma_start(out=wdc[didx % 2][:], in_=wd3[:, :, dc * 128:(dc + 1) * 128]).then_inc(s_wd[didx % 2], 16)

        @blk.sync
        def _(sy):
            for e in range(NEX):
                if e >= 2:
                    sy.wait_ge(s_h, 8 * (e - 1))
                pb_ap = bass.AP(probs_dram, e * TOK, [[0, 128], [1, TOK]])
                sy.dma_start(out=p_bc[e % 2][:], in_=pb_ap).then_inc(s_pbc[e % 2], 16)
            sy.wait_ge(s_x2o, NDC)
            sy.dma_start(out=out.ap().rearrange("(ch p) t -> p ch t", p=128), in_=x2[:]).then_inc(s_out, 16)
            sy.wait_ge(s_out, 16)

        @blk.tensor
        def _(te):
            for e in range(NEX + 1):
                if e < NEX:
                    for fi in range(8):
                        gidx = e * 8 + fi
                        te.wait_ge(s_wg[gidx % 2], 16 * (gidx // 2 + 1))
                        if gidx >= 1:
                            te.wait_ge(s_gc, gidx)
                        for tb in range(2):
                            for ch in range(NCH):
                                mm = te.matmul(GPS[:, tb * QB:(tb + 1) * QB], wgc[gidx % 2][:, ch, :],
                                               flat[:, ch, tb * QB:(tb + 1) * QB],
                                               start=(ch == 0), stop=(ch == NCH - 1))
                        mm.then_inc(s_peg, 1)
                        te.wait_ge(s_wu[gidx % 2], 16 * (gidx // 2 + 1))
                        if gidx >= 1:
                            te.wait_ge(s_uc, gidx)
                        for tb in range(2):
                            for ch in range(NCH):
                                mm = te.matmul(UPS[:, tb * QB:(tb + 1) * QB], wuc[gidx % 2][:, ch, :],
                                               flat[:, ch, tb * QB:(tb + 1) * QB],
                                               start=(ch == 0), stop=(ch == NCH - 1))
                        mm.then_inc(s_peu, 1)
                if e >= 1:
                    ep = e - 1
                    for dc in range(16):
                        didx = ep * 16 + dc
                        te.wait_ge(s_wd[didx % 2], 16 * (didx // 2 + 1))
                        te.wait_ge(s_h, 8 * (ep + 1))
                        if didx >= 2:
                            te.wait_ge(s_x2o, didx - 1)
                        for tb in range(2):
                            for fi in range(8):
                                mm = te.matmul(DPS[:, didx % 2, tb * QB:(tb + 1) * QB],
                                               wdc[didx % 2][:, fi, :],
                                               h_sb[ep % 2][:, fi, tb * QB:(tb + 1) * QB],
                                               start=(fi == 0), stop=(fi == 7))
                        mm.then_inc(s_ped, 1)

        @blk.scalar
        def _(sc):
            for e in range(NEX):
                for fi in range(8):
                    gidx = e * 8 + fi
                    sc.wait_ge(s_gc, gidx + 1)
                    if gidx >= 2:
                        sc.wait_ge(s_h, gidx - 1)
                    sc.activation(out=sg_b[gidx % 2][:], in_=gc_b[gidx % 2][:],
                                  func=AF.Silu).then_inc(s_sg, 1)

        @blk.vector
        def _(v):
            for e in range(NEX + 1):
                if e < NEX:
                    for fi in range(8):
                        gidx = e * 8 + fi
                        v.wait_ge(s_peg, gidx + 1)
                        if gidx >= 2:
                            v.wait_ge(s_sg, gidx - 1)
                        v.tensor_scalar(out=gc_b[gidx % 2][:], in0=GPS[:], scalar1=ALPHA,
                                        scalar2=-ALPHA, op0=ALU.min, op1=ALU.max).then_inc(s_gc, 1)
                        v.wait_ge(s_peu, gidx + 1)
                        v.tensor_scalar(out=uc_b[:], in0=UPS[:], scalar1=ALPHA,
                                        scalar2=-ALPHA, op0=ALU.min, op1=ALU.max).then_inc(s_uc, 1)
                        v.wait_ge(s_sg, gidx + 1)
                        v.tensor_tensor(out=tt_b[:], in0=sg_b[gidx % 2][:], in1=uc_b[:], op=ALU.mult)
                        v.wait_ge(s_pbc[e % 2], 16 * (e // 2 + 1))
                        if e >= 2:
                            v.wait_ge(s_ped, 16 * (e - 1))
                        v.tensor_tensor(out=h_sb[e % 2][:, fi, :], in0=tt_b[:],
                                        in1=p_bc[e % 2][:], op=ALU.mult).then_inc(s_h, 1)
                if e >= 1:
                    ep = e - 1
                    for dc in range(16):
                        didx = ep * 16 + dc
                        v.wait_ge(s_ped, didx + 1)
                        v.tensor_tensor(out=x2[:, dc, :], in0=x2[:, dc, :],
                                        in1=DPS[:, didx % 2, :], op=ALU.add).then_inc(s_x2o, 1)

    es5.close()
    es4.close()
    es.close()
    return nc


def _consts():
    pos = np.arange(S, dtype=np.float64)
    inv = (1.0 / 10000.0 ** (np.arange(0, DH, 2, dtype=np.float32) / DH)).astype(np.float32)
    fr = pos.astype(np.float32)[:, None] * inv[None, :]
    cosv = np.cos(fr).astype(np.float32)
    sinv = np.sin(fr).astype(np.float32)
    ridx = (np.arange(128) % DH) // 2
    cs2 = np.stack([cosv[:, ridx].T, sinv[:, ridx].T], axis=0)
    cs2 = np.ascontiguousarray(cs2, dtype=np.float32)
    k_hi = pos.astype(BF)
    k_lo = (pos - k_hi.astype(np.float64)).astype(BF)
    neg1 = np.full(S, -1.0, dtype=BF)
    kaug = np.stack([k_hi, k_hi, k_lo, neg1, neg1], axis=0)
    r = np.arange(128)[:, None]
    c = np.arange(QB)[None, :]
    dm = np.stack([(128 * o + r <= c) for o in range(4)], axis=1).astype(BF)
    R = np.zeros((DH, DH), dtype=np.float32)
    for i in range(DH // 2):
        R[2 * i, 2 * i + 1] = -1.0
        R[2 * i + 1, 2 * i] = 1.0
    rotT = np.ascontiguousarray(R.T).astype(BF)
    idn_f = np.eye(128, dtype=np.float32)
    idn_b = np.eye(DH, dtype=np.float32).astype(BF)
    ones_f = np.ones((1, 128), dtype=np.float32)
    ones_b = np.ones((128, 1), dtype=np.float32).astype(BF)
    vones = np.zeros((128, 64 * 65), dtype=np.float32)
    vones[:, 64::65] = 1.0
    vones = vones.astype(BF)
    return dict(cs2=cs2, kaug=np.ascontiguousarray(kaug), dm_in=np.ascontiguousarray(dm),
                rotT=rotT, idn_f_in=idn_f, idn_b_in=idn_b, ones_f_in=ones_f,
                ones_b_in=ones_b, vones_in=vones)


def _qaug_for_core(cid):
    pos = np.arange(S, dtype=np.float64)
    rows = []
    for j in range(4):
        hg = 4 * cid + j
        slope = 2.0 ** (-(hg + 1) * 8.0 / 32.0)
        s_hi_b = np.array(slope, dtype=BF)
        s_lo_b = np.array(slope - s_hi_b.astype(np.float64), dtype=BF)
        vv = slope * pos
        v_hi = vv.astype(BF)
        v_lo = (vv - v_hi.astype(np.float64)).astype(BF)
        rows.append(np.stack([
            np.full(S, s_hi_b, dtype=BF),
            np.full(S, s_lo_b, dtype=BF),
            np.full(S, s_hi_b, dtype=BF),
            v_hi, v_lo], axis=0))
    return np.ascontiguousarray(np.stack(rows, axis=0))


_IN_MAPS_HOOK = None


def kernel(L0_attn_hidden_states, L0_attn_Wq, L0_attn_Wk, L0_attn_Wv, L0_attn_Wo,
           L0_attn_bo, L0_ffn_router_w, L0_ffn_expert_bias, L0_ffn_gate_w, L0_ffn_up_w,
           L0_ffn_down_w, L0_ffn_shared_gate_w, L0_ffn_shared_up_w, L0_ffn_shared_down_w):
    f32 = np.float32
    x = np.asarray(L0_attn_hidden_states, f32).reshape(S, D)
    Wq = np.asarray(L0_attn_Wq, f32)
    Wk = np.asarray(L0_attn_Wk, f32)
    Wv = np.asarray(L0_attn_Wv, f32)
    Wo = np.asarray(L0_attn_Wo, f32)
    bo = np.asarray(L0_attn_bo, f32)
    rw = np.asarray(L0_ffn_router_w, f32)
    eb = np.asarray(L0_ffn_expert_bias, f32)
    gw = np.asarray(L0_ffn_gate_w, f32)
    uw = np.asarray(L0_ffn_up_w, f32)
    dw = np.asarray(L0_ffn_down_w, f32)
    sg = np.asarray(L0_ffn_shared_gate_w, f32)
    su = np.asarray(L0_ffn_shared_up_w, f32)
    sd = np.asarray(L0_ffn_shared_down_w, f32)

    x_fm = np.ascontiguousarray(x.T)
    wo_full = np.ascontiguousarray(Wo)
    bo_t2 = np.ascontiguousarray(bo.reshape(NCH, 128).T)
    rwT2 = np.ascontiguousarray(rw.T)
    eb2 = np.ascontiguousarray(eb.reshape(32, 1))
    wg_T = np.ascontiguousarray(np.concatenate([gw.transpose(0, 2, 1), sg.T[None]], axis=0))
    wu_T = np.ascontiguousarray(np.concatenate([uw.transpose(0, 2, 1), su.T[None]], axis=0))
    wd_T = np.ascontiguousarray(np.concatenate([dw.transpose(0, 2, 1), sd.T[None]], axis=0))

    shared = _consts()
    shared.update(x_fm=x_fm, wo=wo_full, bo_t=bo_t2, rwT=rwT2, ebias=eb2,
                  wg_all=wg_T, wu_all=wu_T, wd_all=wd_T)

    in_maps = []
    for c in range(NCORES):
        m = dict(shared)
        m["x_own"] = np.ascontiguousarray(x_fm[:, c * TOK:(c + 1) * TOK])
        m["wq"] = np.ascontiguousarray(Wq[:, c * 4 * DH:(c + 1) * 4 * DH])
        m["wk"] = np.ascontiguousarray(Wk[:, c * DH:(c + 1) * DH])
        m["wv"] = np.ascontiguousarray(Wv[:, c * DH:(c + 1) * DH])
        m["qaug"] = _qaug_for_core(c)
        in_maps.append(m)
    if _IN_MAPS_HOOK is not None:
        return _IN_MAPS_HOOK(in_maps)

    nc = build_program(debug=False)
    res = run_bass_kernel_spmd(nc, in_maps, core_ids=list(range(NCORES)))
    parts = [np.asarray(res.results[c]["out"], dtype=np.float32).T for c in range(NCORES)]
    try:
        nn = [int(np.isnan(p).sum()) for p in parts]
        if any(nn):
            print("KNAN per-core:", nn, flush=True)
            np.save("/tmp/knan_out.npy", np.stack([p[:8, :8] for p in parts]))
    except Exception:
        pass
    return np.concatenate(parts, axis=0).reshape(1, S, D)


# version 09 (final)
# speedup vs baseline: 1.0645x; 1.0463x over previous
import contextlib

import numpy as np
import ml_dtypes

import concourse.bass as bass
from concourse.expressions import s_logical_and
import concourse.mybir as mybir
from concourse.bass_utils import run_bass_kernel_spmd

F32 = mybir.dt.float32
BF16 = mybir.dt.bfloat16
AF = mybir.ActivationFunctionType
ALU = mybir.AluOpType
AX = mybir.AxisListType
BF = ml_dtypes.bfloat16

S = 8192
D = 2048
DH = 64
NCH = 16
TOK = 1024
NCORES = 8
NEX = 33
IFF = 1024
EPS = 1e-5
ALPHA = 7.0

TT1 = 128
NT1 = S // TT1
QB = 512
NQB = S // QB
TKOPS = 17


def _nkc(qb):
    return 4 * (qb + 1)


def _g0(qb):
    return 2 * qb * (qb + 1)


def build_program(debug=False, stop_after=99):
    nc = bass.Bass("TRN2", target_bir_lowering=False)

    x_fm = nc.dram_tensor("x_fm", [D, S], F32, kind="ExternalInput")
    x_own = nc.dram_tensor("x_own", [D, TOK], F32, kind="ExternalInput")
    wq = nc.dram_tensor("wq", [D, 4 * DH], F32, kind="ExternalInput")
    wk = nc.dram_tensor("wk", [D, DH], F32, kind="ExternalInput")
    wv = nc.dram_tensor("wv", [D, DH], F32, kind="ExternalInput")
    wo = nc.dram_tensor("wo", [D, D], F32, kind="ExternalInput")
    bo_t = nc.dram_tensor("bo_t", [128, NCH], F32, kind="ExternalInput")
    rwT = nc.dram_tensor("rwT", [D, 32], F32, kind="ExternalInput")
    ebias = nc.dram_tensor("ebias", [32, 1], F32, kind="ExternalInput")
    wg_all = nc.dram_tensor("wg_all", [NEX, D, IFF], F32, kind="ExternalInput")
    wu_all = nc.dram_tensor("wu_all", [NEX, D, IFF], F32, kind="ExternalInput")
    wd_all = nc.dram_tensor("wd_all", [NEX, IFF, D], F32, kind="ExternalInput")
    cs2 = nc.dram_tensor("cs2", [2, 128, S], F32, kind="ExternalInput")
    kaug = nc.dram_tensor("kaug", [5, S], BF16, kind="ExternalInput")
    qaug = nc.dram_tensor("qaug", [4, 5, S], BF16, kind="ExternalInput")
    dm_in = nc.dram_tensor("dm_in", [128, 4, QB], BF16, kind="ExternalInput")
    rotT = nc.dram_tensor("rotT", [DH, DH], BF16, kind="ExternalInput")
    idn_f_in = nc.dram_tensor("idn_f_in", [128, 128], F32, kind="ExternalInput")
    idn_b_in = nc.dram_tensor("idn_b_in", [DH, DH], BF16, kind="ExternalInput")
    ones_f_in = nc.dram_tensor("ones_f_in", [1, 128], F32, kind="ExternalInput")
    ones_b_in = nc.dram_tensor("ones_b_in", [128, 1], BF16, kind="ExternalInput")
    vones_in = nc.dram_tensor("vones_in", [128, 64 * 65], BF16, kind="ExternalInput")
    out = nc.dram_tensor("out", [D, TOK], F32, kind="ExternalOutput")

    cc_in = nc.dram_tensor("cc_in", [4 * DH, S], BF16, kind="Internal")
    cc_out = nc.dram_tensor("cc_out", [D, S], BF16, kind="Internal")
    probs_dram = nc.dram_tensor("probs_dram", [NEX, TOK], F32, kind="Internal")

    if debug:
        dbg_kaug = nc.dram_tensor("dbg_kaug", [69, S], BF16, kind="ExternalOutput")
        dbg_qaug = nc.dram_tensor("dbg_qaug", [4, 69, S], BF16, kind="ExternalOutput")
        dbg_vaug = nc.dram_tensor("dbg_vaug", [128, 64 * 65], BF16, kind="ExternalOutput")
        dbg_ctx = nc.dram_tensor("dbg_ctx", [4, DH, S], BF16, kind="ExternalOutput")
        dbg_x2 = nc.dram_tensor("dbg_x2", [128, NCH, TOK], F32, kind="ExternalOutput")
        dbg_ctxall = nc.dram_tensor("dbg_ctxall", [128, NCH, TOK], BF16, kind="ExternalOutput")
        dbg_flat = nc.dram_tensor("dbg_flat", [128, NCH, TOK], BF16, kind="ExternalOutput")
        dbg_probs = nc.dram_tensor("dbg_probs", [32, TOK], F32, kind="ExternalOutput")

    es = contextlib.ExitStack()

    def sb(name, shape, dt, stack=None, side=None):
        return (stack or es).enter_context(nc.sbuf_tensor(name, shape, dt, side=side))

    def psum(name, shape, dt, stack=None):
        return (stack or es).enter_context(nc.psum_tensor(name, shape, dt))

    def sem(name):
        return es.enter_context(nc.semaphore(name))

    dm_sb = sb("dm_sb", [128, 4, QB], BF16)
    idn_f = sb("idn_f", [128, 128], F32)
    idn_b = sb("idn_b", [DH, DH], BF16)
    rot_sb = sb("rot_sb", [DH, DH], BF16)
    on65 = sb("on65", [65, 128], F32)
    ones_b = sb("ones_b", [128, 1], BF16)

    esA = contextlib.ExitStack()
    q_aug = [sb(f"q_aug{h}", [69, S], BF16, stack=esA) for h in range(4)]
    k_aug = sb("k_aug", [69, S], BF16, stack=esA)
    v_aug = sb("v_aug", [128, 64 * 65], BF16, stack=esA)

    x_fm3 = x_fm.ap().rearrange("(ch p) s -> p ch s", p=128)
    cs3 = cs2.ap().rearrange("c p s -> p c s")

    es1t = contextlib.ExitStack()
    xt = [sb(f"xt{i}", [128, NCH, TT1], F32, stack=es1t, side="right") for i in range(2)]
    xb = [sb(f"xb{i}", [128, NCH, TT1], BF16, stack=es1t, side="right") for i in range(2)]
    sq = [sb(f"sq{i}", [128, NCH, TT1], BF16, stack=es1t, side="right") for i in range(2)]
    ln_t = [sb(f"ln_t{i}", [128, NCH, TT1], BF16, stack=es1t, side="right") for i in range(2)]
    tmp1 = sb("tmp1", [128, NCH, TT1], F32, stack=es1t, side="right")
    cs_t = [sb(f"cs_t{i}", [128, 2, TT1], F32, stack=es1t, side="right") for i in range(2)]
    wq_sb = sb("wq_sb", [128, NCH, 4 * DH], BF16, stack=es1t, side="right")
    wk_sb = sb("wk_sb", [128, NCH, DH], BF16, stack=es1t, side="right")
    wv_sb = sb("wv_sb", [128, NCH, DH], BF16, stack=es1t, side="right")
    k_bf = sb("k_bf", [DH, TT1], BF16, stack=es1t, side="right")
    v_bf = sb("v_bf", [DH, TT1], BF16, stack=es1t, side="right")
    q_bf = [sb(f"q_bf{i}", [DH, TT1], BF16, stack=es1t, side="right") for i in range(2)]
    t1b = sb("t1b", [DH, TT1], F32, stack=es1t, side="right")
    t2b = sb("t2b", [DH, TT1], F32, stack=es1t, side="right")
    rs_row = [sb(f"rs_row{i}", [1, 2 * TT1], F32, stack=es1t, side="right") for i in range(2)]
    fin_sc = sb("fin_sc", [1, 4 * TT1], F32, stack=es1t, side="right")
    SUM_ps = psum("SUM_ps", [1, TT1], F32, stack=es1t)
    SSQ_ps = psum("SSQ_ps", [1, TT1], F32, stack=es1t)
    RS_ps = psum("RS_ps", [128, 2 * TT1], F32, stack=es1t)
    KP2 = psum("KP2", [DH, 2, TT1], F32, stack=es1t)
    VPS = psum("VPS", [DH, TT1], F32, stack=es1t)
    VT = psum("VT", [128, DH], BF16, stack=es1t)
    QPS = psum("QPS", [DH, TT1], F32, stack=es1t)
    QROT = psum("QROT", [DH, TT1], F32, stack=es1t)

    s_xt = [sem("s_xt0"), sem("s_xt1")]
    s_xb = [sem("s_xb0"), sem("s_xb1")]
    s_sq = sem("s_sq")
    s_cs = [sem("s_cs0"), sem("s_cs1")]
    s_cs_done = sem("s_cs_done")
    s_pe_st = sem("s_pe_st")
    s_fin_ve = sem("s_fin_ve")
    s_act_sd = sem("s_act_sd")
    s_fin = sem("s_fin")
    s_pe_bc1 = sem("s_pe_bc1")
    s_norm = sem("s_norm")
    s_cst = sem("s_cst")
    s_cst_g = sem("s_cst_g")
    s_pe_k = sem("s_pe_k")
    s_pe_kr = sem("s_pe_kr")
    s_pe_v = sem("s_pe_v")
    s_pe_vt = sem("s_pe_vt")
    s_pe_q = sem("s_pe_q")
    s_pe_qr = sem("s_pe_qr")
    s_d_ke = sem("s_d_ke")
    s_d_kc = sem("s_d_kc")
    s_d_ve = sem("s_d_ve")
    s_d_vt = sem("s_d_vt")
    s_d_qe = sem("s_d_qe")
    s_d_qc = sem("s_d_qc")
    s_vch = sem("s_vch")

    VPT1 = 30

    def c_fin_ve(t):
        return VPT1 * t + 5

    def c_fin(t):
        return VPT1 * t + 6

    def c_norm(t):
        return VPT1 * t + 8

    def c_ke(t):
        return VPT1 * t + 9

    def c_kc(t):
        return VPT1 * t + 10

    def c_ve(t):
        return VPT1 * t + 13

    def c_vt(t):
        return VPT1 * t + 14

    def c_qe(qi):
        return VPT1 * (qi // 4) + 15 + 4 * (qi % 4)

    def c_qc(qi):
        return c_qe(qi) + 1

    def c_csd(t):
        return VPT1 * (t + 1)

    with nc.Block() as blk:
        @blk.sync
        def _(sy):
            sy.dma_start(out=dm_sb[:], in_=dm_in.ap()).then_inc(s_cst, 16)
            sy.dma_start(out=idn_f[:], in_=idn_f_in.ap()).then_inc(s_cst, 16)
            sy.dma_start(out=idn_b[:], in_=idn_b_in.ap()).then_inc(s_cst, 16)
            sy.dma_start(out=rot_sb[:], in_=rotT.ap()).then_inc(s_cst, 16)
            sy.dma_start(out=on65[0:1, :], in_=ones_f_in.ap()).then_inc(s_cst, 16)
            sy.dma_start(out=on65[64:65, :], in_=ones_f_in.ap()).then_inc(s_cst, 16)
            sy.dma_start(out=ones_b[:], in_=ones_b_in.ap()).then_inc(s_cst, 16)
            sy.dma_start(out=v_aug[:], in_=vones_in.ap()).then_inc(s_cst, 16)
            sy.dma_start(out=k_aug[64:69, :], in_=kaug.ap()).then_inc(s_cst, 16)
            for h in range(4):
                sy.dma_start(out=q_aug[h][64:69, :], in_=qaug.ap()[h]).then_inc(s_cst, 16)
            for t in range(NT1):
                if t >= 2:
                    sy.wait_ge(s_vch, c_norm(t - 2))
                sy.dma_start(out=xt[t % 2][:], in_=x_fm3[:, :, t * TT1:(t + 1) * TT1]).then_inc(s_xt[t % 2], 16)
                if t >= 2:
                    sy.wait_ge(s_vch, c_csd(t - 2))
                sy.dma_start(out=cs_t[t % 2][:], in_=cs3[:, :, t * TT1:(t + 1) * TT1]).then_inc(s_cs[t % 2], 16)

        @blk.gpsimd
        def _(g):
            g.dma_start(out=wq_sb[:], in_=wq.ap().rearrange("(ch p) f -> p ch f", p=128)).then_inc(s_cst_g, 16)
            g.dma_start(out=wk_sb[:], in_=wk.ap().rearrange("(ch p) f -> p ch f", p=128)).then_inc(s_cst_g, 16)
            g.dma_start(out=wv_sb[:], in_=wv.ap().rearrange("(ch p) f -> p ch f", p=128)).then_inc(s_cst_g, 16)
            for t in range(NT1):
                if t >= 2:
                    g.wait_ge(s_pe_st, t - 1)
                g.dma_start(out=xb[t % 2][:], in_=x_fm3[:, :, t * TT1:(t + 1) * TT1]).then_inc(s_xb[t % 2], 16)

        @blk.scalar
        def _(sc):
            for t in range(NT1):
                sc.wait_ge(s_xt[t % 2], 16 * (t // 2 + 1))
                if t >= 2:
                    sc.wait_ge(s_pe_st, t - 1)
                sc.activation(out=sq[t % 2][:], in_=xt[t % 2][:], func=AF.Square).then_inc(s_sq, 1)
                sc.wait_ge(s_vch, c_fin_ve(t))
                sc.activation(out=fin_sc[0:1, 3 * TT1:4 * TT1], in_=fin_sc[0:1, 2 * TT1:3 * TT1],
                              func=AF.Sqrt).then_inc(s_act_sd, 1)

        @blk.tensor
        def _(te):
            te.wait_ge(s_cst, 16 * 13)
            te.wait_ge(s_cst_g, 48)
            for t in range(NT1):
                par = t % 2
                te.wait_ge(s_xb[t % 2], 16 * (t // 2 + 1))
                te.wait_ge(s_sq, t + 1)
                if t >= 1:
                    te.wait_ge(s_vch, c_fin(t - 1))
                for ch in range(NCH):
                    te.matmul(SUM_ps[:], ones_b[:], xb[par][:, ch, :],
                              start=(ch == 0), stop=(ch == NCH - 1))
                for ch in range(NCH):
                    mm = te.matmul(SSQ_ps[:], ones_b[:], sq[par][:, ch, :],
                                   start=(ch == 0), stop=(ch == NCH - 1))
                mm.then_inc(s_pe_st, 1)
                te.wait_ge(s_vch, c_fin(t))
                if t >= 1:
                    te.wait_ge(s_vch, c_norm(t - 1))
                te.matmul(RS_ps[:], on65[0:1, :], rs_row[par][:],
                          start=True, stop=True).then_inc(s_pe_bc1, 1)
                te.wait_ge(s_vch, c_norm(t))
                if t >= 1:
                    te.wait_ge(s_vch, c_kc(t - 1))
                for ch in range(NCH):
                    mm = te.matmul(KP2[:, 0, :], wk_sb[:, ch, :], ln_t[par][:, ch, :],
                                   start=(ch == 0), stop=(ch == NCH - 1))
                mm.then_inc(s_pe_k, 1)
                te.wait_ge(s_vch, c_ke(t))
                te.matmul(KP2[:, 1, :], rot_sb[:], k_bf[:], start=True, stop=True).then_inc(s_pe_kr, 1)
                if t >= 1:
                    te.wait_ge(s_vch, c_ve(t - 1))
                for ch in range(NCH):
                    mm = te.matmul(VPS[:], wv_sb[:, ch, :], ln_t[par][:, ch, :],
                                   start=(ch == 0), stop=(ch == NCH - 1))
                mm.then_inc(s_pe_v, 1)
                if t >= 1:
                    te.wait_ge(s_vch, c_vt(t - 1))
                te.wait_ge(s_vch, c_ve(t))
                te.transpose(VT[:], v_bf[:], idn_b[:]).then_inc(s_pe_vt, 1)
                for h in range(4):
                    qi = 4 * t + h
                    if qi >= 1:
                        te.wait_ge(s_vch, c_qe(qi - 1))
                    for ch in range(NCH):
                        mm = te.matmul(QPS[:], wq_sb[:, ch, h * DH:(h + 1) * DH],
                                       ln_t[par][:, ch, :],
                                       start=(ch == 0), stop=(ch == NCH - 1))
                    mm.then_inc(s_pe_q, 1)
                    if qi >= 1:
                        te.wait_ge(s_vch, c_qc(qi - 1))
                    te.wait_ge(s_vch, c_qe(qi))
                    te.matmul(QROT[:], rot_sb[:], q_bf[qi % 2][:],
                              start=True, stop=True).then_inc(s_pe_qr, 1)

        @blk.vector
        def _(v):
            vch = [0]

            def chain(op_fn, *incs):
                if vch[0] > 0:
                    v.wait_ge(s_vch, vch[0])
                ins = op_fn()
                ins.then_inc(s_vch, 1)
                for sm, n in incs:
                    ins.then_inc(sm, n)
                vch[0] += 1

            for t in range(NT1):
                par = t % 2
                T0 = t * TT1
                v.wait_ge(s_pe_st, t + 1)
                if t >= 2:
                    v.wait_ge(s_pe_bc1, t - 1)
                chain(lambda par=par: v.tensor_scalar(out=rs_row[par][0:1, 0:TT1], in0=SUM_ps[:],
                                scalar1=-1.0 / D, scalar2=None, op0=ALU.mult))
                chain(lambda: v.tensor_scalar(out=fin_sc[0:1, 0:TT1], in0=SSQ_ps[:],
                                scalar1=1.0 / D, scalar2=None, op0=ALU.mult))
                chain(lambda par=par: v.tensor_tensor(out=fin_sc[0:1, TT1:2 * TT1], in0=rs_row[par][0:1, 0:TT1],
                                in1=rs_row[par][0:1, 0:TT1], op=ALU.mult))
                chain(lambda: v.tensor_tensor(out=fin_sc[0:1, 2 * TT1:3 * TT1], in0=fin_sc[0:1, 0:TT1],
                                in1=fin_sc[0:1, TT1:2 * TT1], op=ALU.subtract))
                chain(lambda: v.tensor_scalar(out=fin_sc[0:1, 2 * TT1:3 * TT1], in0=fin_sc[0:1, 2 * TT1:3 * TT1],
                                scalar1=EPS, scalar2=None, op0=ALU.add))
                v.wait_ge(s_act_sd, t + 1)
                chain(lambda par=par: v.reciprocal(rs_row[par][0:1, TT1:2 * TT1], fin_sc[0:1, 3 * TT1:4 * TT1]))
                v.wait_ge(s_pe_bc1, t + 1)
                if t >= 2:
                    v.wait_ge(s_pe_q, 4 * (t - 2) + 4)
                negmu_b = bass.AP(RS_ps, 0, [[2 * TT1, 128], [0, NCH], [1, TT1]])
                r_b = bass.AP(RS_ps, TT1, [[2 * TT1, 128], [0, NCH], [1, TT1]])
                chain(lambda par=par, negmu_b=negmu_b: v.tensor_tensor(out=tmp1[:], in0=xt[par][:], in1=negmu_b, op=ALU.add))
                chain(lambda par=par, r_b=r_b: v.tensor_tensor(out=ln_t[par][:], in0=tmp1[:], in1=r_b, op=ALU.mult))
                v.wait_ge(s_pe_k, t + 1)
                if t >= 1:
                    v.wait_ge(s_pe_kr, t)
                chain(lambda: v.tensor_copy(k_bf[:], KP2[:, 0, :]))
                v.wait_ge(s_pe_kr, t + 1)
                v.wait_ge(s_cs[t % 2], 16 * (t // 2 + 1))
                chain(lambda par=par: v.tensor_tensor(out=t1b[:], in0=KP2[:, 1, :], in1=cs_t[par][0:DH, 1, :],
                                op=ALU.mult))
                chain(lambda par=par: v.tensor_tensor(out=t2b[:], in0=k_bf[:], in1=cs_t[par][0:DH, 0, :], op=ALU.mult))
                chain(lambda T0=T0: v.tensor_tensor(out=k_aug[0:DH, T0:T0 + TT1], in0=t1b[:], in1=t2b[:], op=ALU.add))
                v.wait_ge(s_pe_v, t + 1)
                if t >= 1:
                    v.wait_ge(s_pe_vt, t)
                chain(lambda: v.tensor_copy(v_bf[:], VPS[:]))
                v.wait_ge(s_pe_vt, t + 1)
                chain(lambda t=t: v.tensor_copy(v_aug[:, t * 65:t * 65 + 64], VT[:]))
                for h in range(4):
                    qi = 4 * t + h
                    v.wait_ge(s_pe_q, qi + 1)
                    if qi >= 2:
                        v.wait_ge(s_pe_qr, qi - 1)
                    chain(lambda qi=qi: v.tensor_copy(q_bf[qi % 2][:], QPS[:]))
                    v.wait_ge(s_pe_qr, qi + 1)
                    chain(lambda par=par: v.tensor_tensor(out=t1b[:], in0=QROT[:], in1=cs_t[par][0:DH, 1, :],
                                    op=ALU.mult))
                    chain(lambda par=par, qi=qi: v.tensor_tensor(out=t2b[:], in0=q_bf[qi % 2][:], in1=cs_t[par][0:DH, 0, :],
                                    op=ALU.mult))
                    chain(lambda h=h, T0=T0: v.tensor_tensor(out=q_aug[h][0:DH, T0:T0 + TT1], in0=t1b[:], in1=t2b[:],
                                         op=ALU.add))

    es1t.close()

    if debug:
        s_dbg1 = sem("s_dbg1")
        with nc.Block() as blk:
            @blk.sync
            def _(sy):
                sy.dma_start(out=dbg_kaug.ap(), in_=k_aug[:, :]).then_inc(s_dbg1, 16)
                for h in range(4):
                    sy.dma_start(out=dbg_qaug.ap()[h], in_=q_aug[h][:, :]).then_inc(s_dbg1, 16)
                sy.dma_start(out=dbg_vaug.ap(), in_=v_aug[:, :]).then_inc(s_dbg1, 16)
                sy.wait_ge(s_dbg1, 16 * 6)


    if stop_after < 2:
        return nc

    es23 = contextlib.ExitStack()
    ctx_sb = [sb(f"ctx{h}", [DH, S], BF16, stack=es23, side="right") for h in range(4)]

    es2 = contextlib.ExitStack()
    ts_sb = sb("ts_sb", [128, 4 * QB], F32, stack=es2, side="right")
    pt_sb = [sb(f"pt{i}", [128, 4 * QB], BF16, stack=es2, side="right") for i in range(2)]
    den_sb = sb("den_sb", [65, QB], F32, stack=es2, side="right")
    rb_sb = sb("rb_sb", [DH, QB], F32, stack=es2, side="right")
    ST = psum("ST", [128, 4 * QB], F32, stack=es2)
    CTX = psum("CTX", [65, 4 * QB], F32, stack=es2)

    s_qk = sem("s_qk")
    s_tanh = sem("s_tanh")
    s_exp = sem("s_exp")
    s_mask = sem("s_mask")
    s_pv = sem("s_pv")
    s_den = sem("s_den")
    s_pe_bc2 = sem("s_pe_bc2")
    s_nrm = sem("s_nrm")
    s_vch2 = sem("s_vch2")

    def _emit_pv(te, qb, kc, G, nkc):
        diag_j = kc - 4 * qb
        if diag_j >= 0:
            te.wait_ge(s_mask, 4 * qb + diag_j + 1)
        else:
            te.wait_ge(s_exp, G + 1)
        if kc == 0 and qb > 0:
            te.wait_ge(s_nrm, 4 * qb)
        par = G % 2
        for h in range(4):
            mm = te.matmul(CTX[:, h * QB:(h + 1) * QB],
                           v_aug[:, kc * 65:kc * 65 + 65],
                           pt_sb[par][:, h * QB:(h + 1) * QB],
                           start=(kc == 0), stop=(kc == nkc - 1))
        mm.then_inc(s_pv, 1)

    with nc.Block() as blk:
        @blk.tensor
        def _(te):
            for qb in range(NQB):
                Q0 = qb * QB
                nkc = _nkc(qb)
                g0 = _g0(qb)
                for kc in range(nkc):
                    G = g0 + kc
                    if G > 0:
                        te.wait_ge(s_tanh, G)
                    if kc == 0 and qb > 0:
                        te.wait_ge(s_nrm, 4 * qb)
                    for h in range(4):
                        mm = te.matmul(ST[:, h * QB:(h + 1) * QB],
                                       k_aug[:, kc * 128:(kc + 1) * 128],
                                       q_aug[h][:, Q0:Q0 + QB], start=True, stop=True)
                    mm.then_inc(s_qk, 1)
                    if kc > 0:
                        _emit_pv(te, qb, kc - 1, G - 1, nkc)
                _emit_pv(te, qb, nkc - 1, g0 + nkc - 1, nkc)
                for h in range(4):
                    te.wait_ge(s_vch2, 8 * qb + 2 * h + 2)
                    te.matmul(ST[0:DH, h * QB:(h + 1) * QB], on65[64:65, 0:DH],
                              den_sb[64:65, :], start=True, stop=True).then_inc(s_pe_bc2, 1)

        @blk.scalar
        def _(sc):
            for qb in range(NQB):
                nkc = _nkc(qb)
                g0 = _g0(qb)
                for kc in range(nkc):
                    G = g0 + kc
                    sc.wait_ge(s_qk, G + 1)
                    sc.activation(out=ts_sb[:], in_=ST[:], func=AF.Tanh,
                                  scale=1.0 / 240.0).then_inc(s_tanh, 1)
                    if G >= 2:
                        sc.wait_ge(s_pv, G - 1)
                    sc.activation(out=pt_sb[G % 2][:], in_=ts_sb[:], func=AF.Exp,
                                  scale=30.0).then_inc(s_exp, 1)

        @blk.vector
        def _(v):
            for qb in range(NQB):
                Q0 = qb * QB
                nkc = _nkc(qb)
                g0 = _g0(qb)
                for j in range(4):
                    G = g0 + 4 * qb + j
                    v.wait_ge(s_exp, G + 1)
                    par = G % 2
                    pt3 = bass.AP(pt_sb[par], 0, [[4 * QB, 128], [QB, 4], [1, QB]])
                    dmj = bass.AP(dm_sb, j * QB, [[4 * QB, 128], [0, 4], [1, QB]])
                    v.tensor_tensor(out=pt3, in0=pt3, in1=dmj, op=ALU.mult).then_inc(s_mask, 1)
                v.wait_ge(s_pv, g0 + nkc)
                for h in range(4):
                    v.tensor_copy(den_sb[64:65, :], CTX[64:65, h * QB:(h + 1) * QB]).then_inc(s_vch2, 1)
                    v.wait_ge(s_vch2, 8 * qb + 2 * h + 1)
                    v.reciprocal(den_sb[64:65, :], den_sb[64:65, :]).then_inc(s_vch2, 1)
                    v.wait_ge(s_pe_bc2, 4 * qb + h + 1)
                    v.tensor_copy(rb_sb[:], ST[0:DH, h * QB:(h + 1) * QB])
                    v.tensor_tensor(out=ctx_sb[h][:, Q0:Q0 + QB],
                                    in0=CTX[0:DH, h * QB:(h + 1) * QB],
                                    in1=rb_sb[:], op=ALU.mult).then_inc(s_nrm, 1)

    es2.close()


    if stop_after < 3:
        return nc

    s_cc1 = sem("s_cc1")
    s_cc2 = sem("s_cc2")
    with nc.Block() as blk:
        @blk.gpsimd
        def _(g):
            for h in range(4):
                g.dma_start(out=cc_in.ap()[h * DH:(h + 1) * DH, :], in_=ctx_sb[h][:, :]).then_inc(s_cc1, 16)
            g.wait_ge(s_cc1, 64)
            g.collective_compute(
                "AllGather", ALU.bypass,
                replica_groups=[list(range(NCORES))],
                ins=[cc_in.ap()], outs=[cc_out.ap()],
            ).then_inc(s_cc2, 1)
            g.wait_ge(s_cc2, 1)

    if debug:
        s_dbg2 = sem("s_dbg2")
        with nc.Block() as blk:
            @blk.sync
            def _(sy):
                for h in range(4):
                    sy.dma_start(out=dbg_ctx.ap()[h], in_=ctx_sb[h][:, :]).then_inc(s_dbg2, 16)
                sy.wait_ge(s_dbg2, 64)

    es23.close()
    esA.close()


    if stop_after < 4:
        return nc

    es4 = contextlib.ExitStack()
    x2 = sb("x2", [128, NCH, TOK], F32, stack=es4, side="right")
    flat = sb("flat", [128, NCH, TOK], BF16, stack=es4, side="right")

    es4a1 = contextlib.ExitStack()
    ctx_all = sb("ctx_all", [128, NCH, TOK], BF16, stack=es4a1, side="right")
    es4a2 = contextlib.ExitStack()
    wo_c = [sb(f"wo_c{i}", [128, NCH, 128], BF16, stack=es4a2, side="right") for i in range(2)]
    bo_sb = sb("bo_sb", [128, NCH], F32, stack=es4a2, side="right")
    ot_sb = sb("ot_sb", [128, QB], F32, stack=es4a2, side="right")
    OPS = psum("OPS", [128, 2, TOK], F32, stack=es4a2)

    s_ctxa = sem("s_ctxa")
    s_xo = sem("s_xo")
    s_wo = [sem("s_wo0"), sem("s_wo1")]
    s_pe_o = sem("s_pe_o")
    s_d_o = sem("s_d_o")
    s_b4c = sem("s_b4c")
    s_xb2 = sem("s_xb2")

    cc_out3 = cc_out.ap().rearrange("(ch p) s -> p ch s", p=128)
    wo3 = wo.ap().rearrange("(ch p) d -> p ch d", p=128)

    with nc.Block() as blk:
        @blk.sync
        def _(sy):
            pid = sy.snap(sy.partition_id(), min_val=0, max_val=NCORES - 1)
            sy.dma_start(out=ctx_all[:],
                         in_=cc_out3[:, :, bass.ds(pid * TOK, TOK)]).then_inc(s_ctxa, 16)
            sy.dma_start(out=x2[:], in_=x_own.ap().rearrange("(ch p) t -> p ch t", p=128)).then_inc(s_xo, 16)
            sy.dma_start(out=bo_sb[:], in_=bo_t.ap()).then_inc(s_b4c, 16)
            if debug:
                sy.wait_ge(s_ctxa, 16)
                sy.dma_start(out=dbg_ctxall.ap(), in_=ctx_all[:]).then_inc(s_b4c, 16)

        @blk.gpsimd
        def _(g):
            for dc in range(NCH):
                if dc >= 2:
                    g.wait_ge(s_pe_o, 2 * (dc - 1))
                g.dma_start(out=wo_c[dc % 2][:], in_=wo3[:, :, dc * 128:(dc + 1) * 128]).then_inc(s_wo[dc % 2], 16)

        @blk.tensor
        def _(te):
            te.wait_ge(s_ctxa, 16)
            for dc in range(NCH):
                te.wait_ge(s_wo[dc % 2], 16 * (dc // 2 + 1))
                if dc >= 2:
                    te.wait_ge(s_d_o, 2 * (dc - 1))
                for tb in range(2):
                    for ch in range(NCH):
                        mm = te.matmul(OPS[:, dc % 2, tb * QB:(tb + 1) * QB],
                                       wo_c[dc % 2][:, ch, :],
                                       ctx_all[:, ch, tb * QB:(tb + 1) * QB],
                                       start=(ch == 0), stop=(ch == NCH - 1))
                    mm.then_inc(s_pe_o, 1)

        @blk.vector
        def _(v):
            v.wait_ge(s_xo, 16)
            v.wait_ge(s_b4c, 16)
            for dc in range(NCH):
                for tb in range(2):
                    v.wait_ge(s_pe_o, 2 * dc + tb + 1)
                    v.tensor_scalar(out=ot_sb[:], in0=OPS[:, dc % 2, tb * QB:(tb + 1) * QB],
                                    scalar1=bo_sb[:, dc:dc + 1], scalar2=None, op0=ALU.add)
                    v.tensor_tensor(out=x2[:, dc, tb * QB:(tb + 1) * QB],
                                    in0=x2[:, dc, tb * QB:(tb + 1) * QB],
                                    in1=ot_sb[:], op=ALU.add).then_inc(s_d_o, 1)
            v.tensor_copy(ctx_all[:], x2[:]).then_inc(s_xb2, 1)

    es4a2.close()


    if stop_after < 5:
        return nc

    es4b = contextlib.ExitStack()
    sq2 = sb("sq2", [128, NCH, TOK], BF16, stack=es4b, side="right")
    tmp4 = sb("tmp4", [128, TOK], F32, stack=es4b, side="right")
    rs2_row = sb("rs2_row", [1, 2 * TOK], F32, stack=es4b, side="right")
    fin2 = sb("fin2", [1, 4 * TOK], F32, stack=es4b, side="right")
    SUM2 = psum("SUM2", [1, TOK], F32, stack=es4b)
    SSQ2 = psum("SSQ2", [1, TOK], F32, stack=es4b)
    RS2 = psum("RS2", [128, 2 * TOK], F32, stack=es4b)

    s_sq2 = sem("s_sq2")
    s_pe_st2 = sem("s_pe_st2")
    s_fin2_ve = sem("s_fin2_ve")
    s_act_sd2 = sem("s_act_sd2")
    s_fin2 = sem("s_fin2")
    s_pe_bc4 = sem("s_pe_bc4")
    s_norm2 = sem("s_norm2")
    s_vch4 = sem("s_vch4")

    with nc.Block() as blk:
        @blk.scalar
        def _(sc):
            sc.activation(out=sq2[:], in_=x2[:], func=AF.Square).then_inc(s_sq2, 1)
            sc.wait_ge(s_fin2_ve, 1)
            sc.activation(out=fin2[0:1, 3 * TOK:4 * TOK], in_=fin2[0:1, 2 * TOK:3 * TOK],
                          func=AF.Sqrt).then_inc(s_act_sd2, 1)

        @blk.tensor
        def _(te):
            for tb in range(2):
                for ch in range(NCH):
                    mm = te.matmul(SUM2[0:1, tb * QB:(tb + 1) * QB], ones_b[:],
                                   ctx_all[:, ch, tb * QB:(tb + 1) * QB],
                                   start=(ch == 0), stop=(ch == NCH - 1))
            mm.then_inc(s_pe_st2, 1)
            te.wait_ge(s_sq2, 1)
            for tb in range(2):
                for ch in range(NCH):
                    mm = te.matmul(SSQ2[0:1, tb * QB:(tb + 1) * QB], ones_b[:],
                                   sq2[:, ch, tb * QB:(tb + 1) * QB],
                                   start=(ch == 0), stop=(ch == NCH - 1))
            mm.then_inc(s_pe_st2, 1)
            te.wait_ge(s_fin2, 1)
            for j in range(4):
                te.matmul(RS2[:, j * QB:(j + 1) * QB], on65[0:1, :],
                          rs2_row[0:1, j * QB:(j + 1) * QB],
                          start=True, stop=True).then_inc(s_pe_bc4, 1)

        @blk.vector
        def _(v):
            v.wait_ge(s_pe_st2, 2)
            v.tensor_scalar(out=rs2_row[0:1, 0:TOK], in0=SUM2[:], scalar1=-1.0 / D,
                            scalar2=None, op0=ALU.mult).then_inc(s_vch4, 1)
            v.tensor_scalar(out=fin2[0:1, 0:TOK], in0=SSQ2[:], scalar1=1.0 / D,
                            scalar2=None, op0=ALU.mult).then_inc(s_vch4, 1)
            v.wait_ge(s_vch4, 2)
            v.tensor_tensor(out=fin2[0:1, TOK:2 * TOK], in0=rs2_row[0:1, 0:TOK],
                            in1=rs2_row[0:1, 0:TOK], op=ALU.mult).then_inc(s_vch4, 1)
            v.wait_ge(s_vch4, 3)
            v.tensor_tensor(out=fin2[0:1, 2 * TOK:3 * TOK], in0=fin2[0:1, 0:TOK],
                            in1=fin2[0:1, TOK:2 * TOK], op=ALU.subtract).then_inc(s_vch4, 1)
            v.wait_ge(s_vch4, 4)
            v.tensor_scalar(out=fin2[0:1, 2 * TOK:3 * TOK], in0=fin2[0:1, 2 * TOK:3 * TOK],
                            scalar1=EPS, scalar2=None, op0=ALU.add).then_inc(s_fin2_ve, 1)
            v.wait_ge(s_act_sd2, 1)
            v.reciprocal(rs2_row[0:1, TOK:2 * TOK], fin2[0:1, 3 * TOK:4 * TOK]).then_inc(s_fin2, 1)
            v.wait_ge(s_pe_bc4, 4)
            for ch in range(NCH):
                v.tensor_tensor(out=tmp4[:], in0=x2[:, ch, :], in1=RS2[:, 0:TOK], op=ALU.add)
                v.tensor_tensor(out=flat[:, ch, :], in0=tmp4[:], in1=RS2[:, TOK:2 * TOK],
                                op=ALU.mult).then_inc(s_norm2, 1)

    es4b.close()
    es4a1.close()


    if stop_after < 6:
        return nc

    es4c = contextlib.ExitStack()
    rw_sb = sb("rw_sb", [128, NCH, 32], BF16, stack=es4c, side="right")
    eb_sb = sb("eb_sb", [32, 1], F32, stack=es4c, side="right")
    sp_a = sb("sp_a", [32, TOK], F32, stack=es4c, side="right")
    sp_r = sb("sp_r", [32, TOK], F32, stack=es4c, side="right")
    sc_sb = sb("sc_sb", [32, TOK], F32, stack=es4c, side="right")
    y_sb = sb("y_sb", [32, TOK], F32, stack=es4c, side="right")
    sc_t = sb("sc_t", [128, 8, 32], F32, stack=es4c, side="right")
    y_t = sb("y_t", [128, 8, 32], F32, stack=es4c, side="right")
    yc_t = sb("yc_t", [128, 8, 32], F32, stack=es4c, side="right")
    et_t = sb("et_t", [128, 32], F32, stack=es4c, side="right")
    mx_t = sb("mx_t", [128, 2], F32, stack=es4c, side="right")
    pr_t = sb("pr_t", [128, 8, 32], F32, stack=es4c, side="right")
    prb = sb("prb", [32, TOK], F32, stack=es4c, side="right")
    ones_row = sb("ones_row", [1, TOK], F32, stack=es4c, side="right")
    RPS = psum("RPS", [32, TOK], F32, stack=es4c)
    TPSa = psum("TPSa", [128, 32], F32, stack=es4c)
    TPSb = psum("TPSb", [128, 32], F32, stack=es4c)
    TPS2 = psum("TPS2", [32, 128], F32, stack=es4c)

    s_b4c2 = sem("s_b4c2")
    s_pe_r = sem("s_pe_r")
    s_sp1 = sem("s_sp1")
    s_spv = sem("s_spv")
    s_scs = sem("s_scs")
    s_yv = sem("s_yv")
    s_pe_t1 = sem("s_pe_t1")
    s_d_t1 = sem("s_d_t1")
    s_tk = sem("s_tk")
    s_pe_t2 = sem("s_pe_t2")
    s_d_t2 = sem("s_d_t2")
    s_prd = sem("s_prd")

    with nc.Block() as blk:
        @blk.gpsimd
        def _(g):
            g.dma_start(out=rw_sb[:], in_=rwT.ap().rearrange("(ch p) e -> p ch e", p=128)).then_inc(s_b4c2, 16)
            g.dma_start(out=eb_sb[:], in_=ebias.ap()).then_inc(s_b4c2, 16)

        @blk.tensor
        def _(te):
            te.wait_ge(s_b4c2, 32)
            for tb in range(2):
                for ch in range(NCH):
                    mm = te.matmul(RPS[:, tb * QB:(tb + 1) * QB], rw_sb[:, ch, :],
                                   flat[:, ch, tb * QB:(tb + 1) * QB],
                                   start=(ch == 0), stop=(ch == NCH - 1))
                mm.then_inc(s_pe_r, 1)
            te.wait_ge(s_yv, 1)
            for b in range(8):
                if b >= 1:
                    te.wait_ge(s_d_t1, b)
                te.transpose(TPSa[:], sc_sb[:, b * 128:(b + 1) * 128], idn_f[0:32, 0:32])
                te.transpose(TPSb[:], y_sb[:, b * 128:(b + 1) * 128],
                             idn_f[0:32, 0:32]).then_inc(s_pe_t1, 1)
            for b in range(8):
                te.wait_ge(s_tk, TKOPS * (b + 1))
                if b >= 1:
                    te.wait_ge(s_d_t2, b)
                te.transpose(TPS2[:], pr_t[:, b, :], idn_f[:, :]).then_inc(s_pe_t2, 1)

        @blk.scalar
        def _(sc):
            sc.wait_ge(s_pe_r, 2)
            sc.activation(out=sp_a[:], in_=RPS[:], func=AF.Abs)
            sc.activation(out=sp_a[:], in_=sp_a[:], func=AF.Exp, scale=-1.0)
            sc.activation(out=sp_a[:], in_=sp_a[:], func=AF.Ln, bias=1.0)
            sc.activation(out=sp_r[:], in_=RPS[:], func=AF.Relu).then_inc(s_sp1, 1)
            sc.wait_ge(s_spv, 1)
            sc.activation(out=sc_sb[:], in_=sc_sb[:], func=AF.Sqrt).then_inc(s_scs, 1)

        @blk.vector
        def _(v):
            v.wait_ge(s_sp1, 1)
            v.tensor_tensor(out=sc_sb[:], in0=sp_a[:], in1=sp_r[:], op=ALU.add).then_inc(s_spv, 1)
            v.wait_ge(s_scs, 1)
            v.wait_ge(s_b4c2, 32)
            v.tensor_scalar(out=y_sb[:], in0=sc_sb[:], scalar1=eb_sb[:, 0:1], scalar2=None,
                            op0=ALU.add).then_inc(s_yv, 1)
            for b in range(8):
                v.wait_ge(s_pe_t1, b + 1)
                v.tensor_copy(sc_t[:, b, :], TPSa[:])
                v.tensor_copy(y_t[:, b, :], TPSb[:]).then_inc(s_d_t1, 1)
            tk = [0]

            def step(op_fn):
                if tk[0] > 0:
                    v.wait_ge(s_tk, tk[0])
                op_fn().then_inc(s_tk, 1)
                tk[0] += 1

            for b in range(8):
                yb = y_t[:, b, :]
                step(lambda b=b, yb=yb: v.tensor_copy(yc_t[:, b, :], yb))
                for it in range(4):
                    dst = mx_t[:, 0:1] if it < 3 else mx_t[:, 1:2]
                    step(lambda dst=dst, yb=yb: v.tensor_reduce(dst, yb, axis=AX.X, op=ALU.max))
                    if it < 3:
                        step(lambda yb=yb: v.tensor_scalar(out=et_t[:], in0=yb, scalar1=mx_t[:, 0:1],
                                                           scalar2=None, op0=ALU.is_equal))
                        step(lambda yb=yb: v.scalar_tensor_tensor(out=yb, in0=et_t[:], scalar=-1e30,
                                                                  in1=yb, op0=ALU.mult, op1=ALU.add))
                step(lambda b=b: v.tensor_scalar(out=et_t[:], in0=yc_t[:, b, :], scalar1=mx_t[:, 1:2],
                                                 scalar2=None, op0=ALU.is_ge))
                step(lambda b=b: v.tensor_tensor(out=pr_t[:, b, :], in0=et_t[:], in1=sc_t[:, b, :],
                                                 op=ALU.mult))
                step(lambda b=b: v.tensor_reduce(mx_t[:, 0:1], pr_t[:, b, :], axis=AX.X, op=ALU.add))
                step(lambda: v.tensor_scalar(out=mx_t[:, 0:1], in0=mx_t[:, 0:1], scalar1=1e-20,
                                             scalar2=None, op0=ALU.add))
                step(lambda: v.reciprocal(mx_t[:, 0:1], mx_t[:, 0:1]))
                step(lambda b=b: v.tensor_scalar(out=pr_t[:, b, :], in0=pr_t[:, b, :],
                                                 scalar1=mx_t[:, 0:1], scalar2=None, op0=ALU.mult))
            assert tk[0] == TKOPS * 8, tk[0]
            for b in range(8):
                v.wait_ge(s_pe_t2, b + 1)
                v.tensor_copy(prb[:, b * 128:(b + 1) * 128], TPS2[:]).then_inc(s_d_t2, 1)
            v.memset(ones_row[:], 1.0).then_inc(s_d_t2, 1)

        @blk.sync
        def _(sy):
            sy.wait_ge(s_d_t2, 9)
            sy.dma_start(out=probs_dram.ap()[0:32, :], in_=prb[:, :]).then_inc(s_prd, 16)
            sy.dma_start(out=probs_dram.ap()[32:33, :], in_=ones_row[:, :]).then_inc(s_prd, 16)
            sy.wait_ge(s_prd, 32)

    es4c.close()

    if debug:
        s_dbg4 = sem("s_dbg4")
        with nc.Block() as blk:
            @blk.sync
            def _(sy):
                sy.dma_start(out=dbg_x2.ap(), in_=x2[:]).then_inc(s_dbg4, 16)
                sy.dma_start(out=dbg_flat.ap(), in_=flat[:]).then_inc(s_dbg4, 16)
                sy.dma_start(out=dbg_probs.ap(), in_=probs_dram.ap()[0:32, :]).then_inc(s_dbg4, 16)
                sy.wait_ge(s_dbg4, 48)


    if stop_after < 7:
        return nc

    es5 = contextlib.ExitStack()
    h_sb = [sb(f"h_sb{i}", [128, 8, TOK], BF16, stack=es5, side="right") for i in range(2)]
    wgc = [sb(f"wgc{i}", [128, NCH, 128], BF16, stack=es5, side="right") for i in range(4)]
    wuc = [sb(f"wuc{i}", [128, NCH, 128], BF16, stack=es5, side="right") for i in range(4)]
    wdc = [sb(f"wdc{i}", [128, 8, 128], BF16, stack=es5, side="right") for i in range(4)]
    gc_b = [sb(f"gc_b{i}", [128, TOK], F32, stack=es5, side="right") for i in range(2)]
    sg_b = [sb(f"sg_b{i}", [128, TOK], F32, stack=es5, side="right") for i in range(2)]
    uc_b = sb("uc_b", [128, TOK], F32, stack=es5, side="right")
    tt_b = sb("tt_b", [128, TOK], F32, stack=es5, side="right")
    p_bc = [sb(f"p_bc{i}", [128, TOK], F32, stack=es5, side="right") for i in range(2)]
    GPS = psum("GPS", [128, TOK], F32, stack=es5)
    UPS = psum("UPS", [128, TOK], F32, stack=es5)
    DPS = psum("DPS", [128, 2, TOK], F32, stack=es5)

    s_wg = [sem(f"s_wg{i}") for i in range(4)]
    s_wu = [sem(f"s_wu{i}") for i in range(4)]
    s_wd = [sem(f"s_wd{i}") for i in range(4)]
    s_pbc = [sem("s_pbc0"), sem("s_pbc1")]
    s_peg = sem("s_peg")
    s_peu = sem("s_peu")
    s_ped = sem("s_ped")
    s_gc = sem("s_gc")
    s_uc = sem("s_uc")
    s_sg = sem("s_sg")
    s_h = sem("s_h")
    s_x2o = sem("s_x2o")
    s_out = sem("s_out")

    NDC = NEX * 16

    with nc.Block() as blk:
        @blk.gpsimd
        def _(g):
            for e in range(NEX + 1):
                if e < NEX:
                    wg3 = wg_all.ap()[e].rearrange("(ch p) f -> p ch f", p=128)
                    wu3 = wu_all.ap()[e].rearrange("(ch p) f -> p ch f", p=128)
                    for fi in range(8):
                        gidx = e * 8 + fi
                        if gidx >= 4:
                            g.wait_ge(s_peg, gidx - 3)
                        g.dma_start(out=wgc[gidx % 4][:], in_=wg3[:, :, fi * 128:(fi + 1) * 128]).then_inc(s_wg[gidx % 4], 16)
                        if gidx >= 4:
                            g.wait_ge(s_peu, gidx - 3)
                        g.dma_start(out=wuc[gidx % 4][:], in_=wu3[:, :, fi * 128:(fi + 1) * 128]).then_inc(s_wu[gidx % 4], 16)
                if e >= 1:
                    wd3 = wd_all.ap()[e - 1].rearrange("(f p) d -> p f d", p=128)
                    for dc in range(16):
                        didx = (e - 1) * 16 + dc
                        if didx >= 4:
                            g.wait_ge(s_ped, didx - 3)
                        g.dma_start(out=wdc[didx % 4][:], in_=wd3[:, :, dc * 128:(dc + 1) * 128]).then_inc(s_wd[didx % 4], 16)

        @blk.sync
        def _(sy):
            for e in range(NEX):
                if e >= 2:
                    sy.wait_ge(s_h, 8 * (e - 1))
                pb_ap = bass.AP(probs_dram, e * TOK, [[0, 128], [1, TOK]])
                sy.dma_start(out=p_bc[e % 2][:], in_=pb_ap).then_inc(s_pbc[e % 2], 16)
            sy.wait_ge(s_x2o, NDC)
            sy.dma_start(out=out.ap().rearrange("(ch p) t -> p ch t", p=128), in_=x2[:]).then_inc(s_out, 16)
            sy.wait_ge(s_out, 16)

        @blk.tensor
        def _(te):
            for e in range(NEX + 1):
                if e < NEX:
                    for fi in range(8):
                        gidx = e * 8 + fi
                        te.wait_ge(s_wg[gidx % 4], 16 * (gidx // 4 + 1))
                        if gidx >= 1:
                            te.wait_ge(s_gc, gidx)
                        for tb in range(2):
                            for ch in range(NCH):
                                mm = te.matmul(GPS[:, tb * QB:(tb + 1) * QB], wgc[gidx % 4][:, ch, :],
                                               flat[:, ch, tb * QB:(tb + 1) * QB],
                                               start=(ch == 0), stop=(ch == NCH - 1))
                        mm.then_inc(s_peg, 1)
                        te.wait_ge(s_wu[gidx % 4], 16 * (gidx // 4 + 1))
                        if gidx >= 1:
                            te.wait_ge(s_uc, gidx)
                        for tb in range(2):
                            for ch in range(NCH):
                                mm = te.matmul(UPS[:, tb * QB:(tb + 1) * QB], wuc[gidx % 4][:, ch, :],
                                               flat[:, ch, tb * QB:(tb + 1) * QB],
                                               start=(ch == 0), stop=(ch == NCH - 1))
                        mm.then_inc(s_peu, 1)
                if e >= 1:
                    ep = e - 1
                    for dc in range(16):
                        didx = ep * 16 + dc
                        te.wait_ge(s_wd[didx % 4], 16 * (didx // 4 + 1))
                        te.wait_ge(s_h, 8 * (ep + 1))
                        if didx >= 2:
                            te.wait_ge(s_x2o, didx - 1)
                        for tb in range(2):
                            for fi in range(8):
                                mm = te.matmul(DPS[:, didx % 2, tb * QB:(tb + 1) * QB],
                                               wdc[didx % 4][:, fi, :],
                                               h_sb[ep % 2][:, fi, tb * QB:(tb + 1) * QB],
                                               start=(fi == 0), stop=(fi == 7))
                        mm.then_inc(s_ped, 1)

        @blk.scalar
        def _(sc):
            for e in range(NEX):
                for fi in range(8):
                    gidx = e * 8 + fi
                    sc.wait_ge(s_gc, gidx + 1)
                    if gidx >= 2:
                        sc.wait_ge(s_h, gidx - 1)
                    sc.activation(out=sg_b[gidx % 2][:], in_=gc_b[gidx % 2][:],
                                  func=AF.Silu).then_inc(s_sg, 1)

        @blk.vector
        def _(v):
            for e in range(NEX + 1):
                if e < NEX:
                    for fi in range(8):
                        gidx = e * 8 + fi
                        v.wait_ge(s_peg, gidx + 1)
                        if gidx >= 2:
                            v.wait_ge(s_sg, gidx - 1)
                        v.tensor_scalar(out=gc_b[gidx % 2][:], in0=GPS[:], scalar1=ALPHA,
                                        scalar2=-ALPHA, op0=ALU.min, op1=ALU.max).then_inc(s_gc, 1)
                        v.wait_ge(s_peu, gidx + 1)
                        v.tensor_scalar(out=uc_b[:], in0=UPS[:], scalar1=ALPHA,
                                        scalar2=-ALPHA, op0=ALU.min, op1=ALU.max).then_inc(s_uc, 1)
                        v.wait_ge(s_sg, gidx + 1)
                        v.tensor_tensor(out=tt_b[:], in0=sg_b[gidx % 2][:], in1=uc_b[:], op=ALU.mult)
                        v.wait_ge(s_pbc[e % 2], 16 * (e // 2 + 1))
                        if e >= 2:
                            v.wait_ge(s_ped, 16 * (e - 1))
                        v.tensor_tensor(out=h_sb[e % 2][:, fi, :], in0=tt_b[:],
                                        in1=p_bc[e % 2][:], op=ALU.mult).then_inc(s_h, 1)
                if e >= 1:
                    ep = e - 1
                    for dc in range(16):
                        didx = ep * 16 + dc
                        v.wait_ge(s_ped, didx + 1)
                        v.tensor_tensor(out=x2[:, dc, :], in0=x2[:, dc, :],
                                        in1=DPS[:, didx % 2, :], op=ALU.add).then_inc(s_x2o, 1)

    es5.close()
    es4.close()
    es.close()
    return nc


def _consts():
    pos = np.arange(S, dtype=np.float64)
    inv = (1.0 / 10000.0 ** (np.arange(0, DH, 2, dtype=np.float32) / DH)).astype(np.float32)
    fr = pos.astype(np.float32)[:, None] * inv[None, :]
    cosv = np.cos(fr).astype(np.float32)
    sinv = np.sin(fr).astype(np.float32)
    ridx = (np.arange(128) % DH) // 2
    cs2 = np.stack([cosv[:, ridx].T, sinv[:, ridx].T], axis=0)
    cs2 = np.ascontiguousarray(cs2, dtype=np.float32)
    k_hi = pos.astype(BF)
    k_lo = (pos - k_hi.astype(np.float64)).astype(BF)
    neg1 = np.full(S, -1.0, dtype=BF)
    kaug = np.stack([k_hi, k_hi, k_lo, neg1, neg1], axis=0)
    r = np.arange(128)[:, None]
    c = np.arange(QB)[None, :]
    dm = np.stack([(128 * o + r <= c) for o in range(4)], axis=1).astype(BF)
    R = np.zeros((DH, DH), dtype=np.float32)
    for i in range(DH // 2):
        R[2 * i, 2 * i + 1] = -1.0
        R[2 * i + 1, 2 * i] = 1.0
    rotT = np.ascontiguousarray(R.T).astype(BF)
    idn_f = np.eye(128, dtype=np.float32)
    idn_b = np.eye(DH, dtype=np.float32).astype(BF)
    ones_f = np.ones((1, 128), dtype=np.float32)
    ones_b = np.ones((128, 1), dtype=np.float32).astype(BF)
    vones = np.zeros((128, 64 * 65), dtype=np.float32)
    vones[:, 64::65] = 1.0
    vones = vones.astype(BF)
    return dict(cs2=cs2, kaug=np.ascontiguousarray(kaug), dm_in=np.ascontiguousarray(dm),
                rotT=rotT, idn_f_in=idn_f, idn_b_in=idn_b, ones_f_in=ones_f,
                ones_b_in=ones_b, vones_in=vones)


def _qaug_for_core(cid):
    pos = np.arange(S, dtype=np.float64)
    rows = []
    for j in range(4):
        hg = 4 * cid + j
        slope = 2.0 ** (-(hg + 1) * 8.0 / 32.0)
        s_hi_b = np.array(slope, dtype=BF)
        s_lo_b = np.array(slope - s_hi_b.astype(np.float64), dtype=BF)
        vv = slope * pos
        v_hi = vv.astype(BF)
        v_lo = (vv - v_hi.astype(np.float64)).astype(BF)
        rows.append(np.stack([
            np.full(S, s_hi_b, dtype=BF),
            np.full(S, s_lo_b, dtype=BF),
            np.full(S, s_hi_b, dtype=BF),
            v_hi, v_lo], axis=0))
    return np.ascontiguousarray(np.stack(rows, axis=0))


_IN_MAPS_HOOK = None


def kernel(L0_attn_hidden_states, L0_attn_Wq, L0_attn_Wk, L0_attn_Wv, L0_attn_Wo,
           L0_attn_bo, L0_ffn_router_w, L0_ffn_expert_bias, L0_ffn_gate_w, L0_ffn_up_w,
           L0_ffn_down_w, L0_ffn_shared_gate_w, L0_ffn_shared_up_w, L0_ffn_shared_down_w):
    f32 = np.float32
    x = np.asarray(L0_attn_hidden_states, f32).reshape(S, D)
    Wq = np.asarray(L0_attn_Wq, f32)
    Wk = np.asarray(L0_attn_Wk, f32)
    Wv = np.asarray(L0_attn_Wv, f32)
    Wo = np.asarray(L0_attn_Wo, f32)
    bo = np.asarray(L0_attn_bo, f32)
    rw = np.asarray(L0_ffn_router_w, f32)
    eb = np.asarray(L0_ffn_expert_bias, f32)
    gw = np.asarray(L0_ffn_gate_w, f32)
    uw = np.asarray(L0_ffn_up_w, f32)
    dw = np.asarray(L0_ffn_down_w, f32)
    sg = np.asarray(L0_ffn_shared_gate_w, f32)
    su = np.asarray(L0_ffn_shared_up_w, f32)
    sd = np.asarray(L0_ffn_shared_down_w, f32)

    x_fm = np.ascontiguousarray(x.T)
    wo_full = np.ascontiguousarray(Wo)
    bo_t2 = np.ascontiguousarray(bo.reshape(NCH, 128).T)
    rwT2 = np.ascontiguousarray(rw.T)
    eb2 = np.ascontiguousarray(eb.reshape(32, 1))
    wg_T = np.ascontiguousarray(np.concatenate([gw.transpose(0, 2, 1), sg.T[None]], axis=0))
    wu_T = np.ascontiguousarray(np.concatenate([uw.transpose(0, 2, 1), su.T[None]], axis=0))
    wd_T = np.ascontiguousarray(np.concatenate([dw.transpose(0, 2, 1), sd.T[None]], axis=0))

    shared = _consts()
    shared.update(x_fm=x_fm, wo=wo_full, bo_t=bo_t2, rwT=rwT2, ebias=eb2,
                  wg_all=wg_T, wu_all=wu_T, wd_all=wd_T)

    in_maps = []
    for c in range(NCORES):
        m = dict(shared)
        m["x_own"] = np.ascontiguousarray(x_fm[:, c * TOK:(c + 1) * TOK])
        m["wq"] = np.ascontiguousarray(Wq[:, c * 4 * DH:(c + 1) * 4 * DH])
        m["wk"] = np.ascontiguousarray(Wk[:, c * DH:(c + 1) * DH])
        m["wv"] = np.ascontiguousarray(Wv[:, c * DH:(c + 1) * DH])
        m["qaug"] = _qaug_for_core(c)
        in_maps.append(m)
    if _IN_MAPS_HOOK is not None:
        return _IN_MAPS_HOOK(in_maps)

    nc = build_program(debug=False)
    res = run_bass_kernel_spmd(nc, in_maps, core_ids=list(range(NCORES)))
    parts = [np.asarray(res.results[c]["out"], dtype=np.float32).T for c in range(NCORES)]
    try:
        nn = [int(np.isnan(p).sum()) for p in parts]
        if any(nn):
            print("KNAN per-core:", nn, flush=True)
            np.save("/tmp/knan_out.npy", np.stack([p[:8, :8] for p in parts]))
    except Exception:
        pass
    return np.concatenate(parts, axis=0).reshape(1, S, D)
